# version 07
# baseline (speedup 1.0000x reference)
import contextlib

import numpy as np
import ml_dtypes
import concourse.bass as bass
import concourse.mybir as mybir
from concourse.bass_utils import run_bass_kernel_spmd

F32 = mybir.dt.float32
BF16 = mybir.dt.bfloat16
AF = mybir.ActivationFunctionType
ALU = mybir.AluOpType
AX = mybir.AxisListType
BF = ml_dtypes.bfloat16

NCORES = 8
HALO = 8
DM = 1024
DI = 2048
DS = 16
DTR = 64
NE = 32
HE = 256
NS = 2
HS = 1024
NDM = DM // 128
NDI = DI // 128


class Buf:
    def __init__(self, h):
        self.h = h
        self.w = None
        self.readers = []

    def __getitem__(self, idx):
        return self.h[idx]


class Prog:
    def __init__(self):
        self.nc = bass.Bass("TRN2", target_bir_lowering=False)
        self.stack = contextlib.ExitStack()
        self.alloc_stack = self.stack
        self.ops = {k: [] for k in ("t", "v", "s", "gc", "g", "q")}
        self.count = {k: 0 for k in self.ops}
        self.sems = {}
        self.waited = {k: {} for k in self.ops}
        self._pending = {}
        self._names = {}
        for k in self.ops:
            self._sem(k)

    def _uname(self, base):
        i = self._names.get(base, 0)
        self._names[base] = i + 1
        return base if i == 0 else f"{base}_{i}"

    def sbuf(self, base, shape, dtype):
        return Buf(self.alloc_stack.enter_context(
            self.nc.sbuf_tensor(self._uname(base), shape, dtype)))

    def psum(self, base, shape, dtype=F32):
        return Buf(self.alloc_stack.enter_context(
            self.nc.psum_tensor(self._uname(base), shape, dtype)))

    def dram(self, base, shape, dtype, kind="Internal"):
        return self.nc.dram_tensor(self._uname(base), shape, dtype, kind=kind)

    @contextlib.contextmanager
    def scope(self):
        prev = self.alloc_stack
        self.alloc_stack = contextlib.ExitStack()
        try:
            yield
        finally:
            self.barrier()
            self.alloc_stack.close()
            self.alloc_stack = prev

    def _sem(self, key):
        if key not in self.sems:
            self.sems[key] = self.stack.enter_context(
                self.nc.semaphore(f"sem_{key}"))
        return self.sems[key]

    ENG_STREAM = {"t": "t", "v": "v", "s": "s", "gc": "g", "g": "g", "q": "q"}
    ENG_STEP = {"t": 1, "v": 1, "s": 1, "gc": 1, "g": 16, "q": 16}

    def raw_emit(self, eng, fn, deps=()):
        sem = self._sem(eng)
        step = self.ENG_STEP[eng]
        stream = self.ENG_STREAM[eng]
        deps = list(deps) + self._pending.pop(eng, [])
        if eng in ("g", "q"):
            deps.append((eng, self.count[eng]))
        waits = []
        for d in deps:
            if d is None:
                continue
            dk, dv = d
            if self.waited[eng].get(dk, -1) >= dv:
                continue
            if dk == eng and dv > self.count[eng]:
                raise RuntimeError("self-dep on future op")
            waits.append((dk, dv))
            self.waited[eng][dk] = dv

        def run(e, waits=tuple(waits), fn=fn, sem=sem, step=step):
            for dk, dv in waits:
                e.wait_ge(self._sem(dk), dv)
            fn(e).then_inc(sem, step)

        self.ops[stream].append(run)
        self.count[eng] += step
        return (eng, self.count[eng])

    def do(self, eng, fn, reads=(), writes=(), deps=()):
        d = list(deps)
        for b in reads:
            if b.w:
                d.append(b.w)
        for b in writes:
            d.extend(b.readers)
            if b.w:
                d.append(b.w)
        tok = self.raw_emit(eng, fn, deps=d)
        for b in reads:
            b.readers.append(tok)
        for b in writes:
            b.w = tok
            b.readers = []
        return tok

    def do_group(self, eng, fns, reads=(), writes=(), deps=()):
        assert eng not in ("g", "q")
        d = list(deps) + self._pending.pop(eng, [])
        for b in reads:
            if b.w:
                d.append(b.w)
        for b in writes:
            d.extend(b.readers)
            if b.w:
                d.append(b.w)
        sem = self._sem(eng)
        waits = []
        for dd in d:
            if dd is None:
                continue
            dk, dv = dd
            if self.waited[eng].get(dk, -1) >= dv:
                continue
            if dk == eng and dv > self.count[eng]:
                raise RuntimeError("self-dep on future op")
            waits.append((dk, dv))
            self.waited[eng][dk] = dv
        n = len(fns)

        for i, fn in enumerate(fns):
            def run(e, fn=fn, w=tuple(waits) if i == 0 else (), last=(i == n - 1),
                    sem=sem):
                for dk, dv in w:
                    e.wait_ge(self._sem(dk), dv)
                inst = fn(e)
                if last:
                    inst.then_inc(sem, 1)
            self.ops[self.ENG_STREAM[eng]].append(run)
        self.count[eng] += 1
        tok = (eng, self.count[eng])
        for b in reads:
            b.readers.append(tok)
        for b in writes:
            b.w = tok
            b.readers = []
        return tok

    def dma(self, queue, out_ap, in_ap, reads=(), writes=(), deps=(), **kw):
        return self.do(queue, lambda e: e.dma_start(out=out_ap, in_=in_ap, **kw),
                       reads=reads, writes=writes, deps=deps)

    def collective(self, fn):
        key = f"cc{sum(1 for k in self.sems if k.startswith('cc'))}"
        sem = self._sem(key)
        deps = [("g", self.count["g"])] + self._pending.pop("g", [])
        waits = []
        for dk, dv in deps:
            if self.waited["g"].get(dk, -1) >= dv:
                continue
            waits.append((dk, dv))
            self.waited["g"][dk] = dv

        def run(e, waits=tuple(waits), fn=fn, sem=sem):
            for dk, dv in waits:
                e.wait_ge(self._sem(dk), dv)
            fn(e).then_inc(sem)

        self.ops["g"].append(run)
        tok = (key, 1)
        for k in ("t", "v", "s", "gc", "g", "q"):
            self._pending.setdefault(k, []).append(tok)
        return tok

    def barrier(self):
        toks = [(k, self.count[k]) for k in ("t", "v", "s", "gc", "g", "q")
                if self.count[k] > 0]
        for k in ("t", "v", "s", "gc", "g", "q"):
            self._pending.setdefault(k, []).extend(toks)

    def finalize(self, final_waits=()):
        nc = self.nc
        fw = [t for t in final_waits if t is not None]
        with nc.Block() as block:
            @block.tensor
            def _(e):
                for fn in self.ops["t"]:
                    fn(e)

            @block.vector
            def _(e):
                for fn in self.ops["v"]:
                    fn(e)

            @block.scalar
            def _(e):
                for fn in self.ops["s"]:
                    fn(e)

            @block.sync
            def _(e):
                for fn in self.ops["q"]:
                    fn(e)

            @block.gpsimd
            def _(e):
                for fn in self.ops["g"]:
                    fn(e)
                for dk, dv in fw:
                    e.wait_ge(self._sem(dk), dv)
        self.stack.close()
        return nc


def chunk_plan(T):
    out = []
    off = 0
    while off < T:
        c = min(512, T - off)
        out.append((off, c))
        off += c
    return out


def make_consts(p):
    c = {}
    nc = p.nc
    c["ident_bf"] = p.sbuf("ident_bf", [128, 128], BF16)
    c["ident_f"] = p.sbuf("ident_f", [128, 128], F32)
    i_bf = nc.dram_tensor("c_ident_bf", [128, 128], BF16, kind="ExternalInput")
    i_f = nc.dram_tensor("c_ident_f", [128, 128], F32, kind="ExternalInput")
    p.dma("q", c["ident_bf"][:], i_bf[:, :], writes=[c["ident_bf"]])
    p.dma("q", c["ident_f"][:], i_f[:, :], writes=[c["ident_f"]])
    o_bf = nc.dram_tensor("c_ones", [128, 128], BF16, kind="ExternalInput")
    c["ones_sq"] = p.sbuf("ones_sq", [128, 128], BF16)
    p.dma("q", c["ones_sq"][:], o_bf[:, :], writes=[c["ones_sq"]])
    cc = nc.dram_tensor("c_cols", [128, 2], F32, kind="ExternalInput")
    ccs = p.sbuf("c_cols_sb", [128, 2], F32)
    p.dma("q", ccs[:], cc[:, :], writes=[ccs])
    c["one"] = _SubBuf(ccs, 0)
    c["negone"] = _SubBuf(ccs, 1)
    return c


class _SubBuf:

    def __init__(self, buf, col):
        self._b = buf
        self._c = col

    def __getitem__(self, idx):
        ps, fs = idx
        return self._b.h[ps, self._c + fs.start:self._c + fs.stop]

    @property
    def w(self):
        return self._b.w

    @w.setter
    def w(self, v):
        self._b.w = v

    @property
    def readers(self):
        return self._b.readers

    @readers.setter
    def readers(self, v):
        self._b.readers = v


def host_consts():
    return {
        "c_ident_bf": np.eye(128, dtype=BF),
        "c_ident_f": np.eye(128, dtype=np.float32),
        "c_ones": np.ones((128, 128), dtype=BF),
        "c_cols": np.stack([np.ones(128, np.float32), -np.ones(128, np.float32)], axis=1),
    }


def emit_ln(p, cst, C, x_tiles, xn_tiles, nd=NDM, ps=None, scr=None):
    D = nd * 128
    ps_red, ps_rep = ps
    xb, sq, mean_r, rstd_r, rowb = (scr["xb"], scr["sq"], scr["mean"],
                                    scr["rstd"], scr["rowb"])
    for i in range(nd):
        p.do("s", lambda e, i=i: e.copy(xb[:, 0:C], x_tiles[i][0:128, 0:C]),
             reads=[x_tiles[i]], writes=[xb])
        p.do("t", lambda e, i=i: e.matmul(
            ps_red[0:1, 0:C], cst["ones_sq"][0:128, 0:1], xb[:, 0:C],
            start=(i == 0), stop=(i == nd - 1)),
            reads=[xb], writes=[ps_red])
    p.do("v", lambda e: e.tensor_scalar_mul(mean_r[0:1, 0:C], ps_red[0:1, 0:C], 1.0 / D),
         reads=[ps_red], writes=[mean_r])
    p.do("v", lambda e: e.tensor_copy(rowb[0:1, 0:C], mean_r[0:1, 0:C]),
         reads=[mean_r], writes=[rowb])
    p.do("t", lambda e: e.matmul(ps_rep[:, 0:C], cst["ones_sq"][0:1, 0:128],
                                 rowb[0:1, 0:C], start=True, stop=True),
         reads=[rowb], writes=[ps_rep])
    for i in range(nd):
        p.do("v", lambda e, i=i: e.tensor_sub(
            xn_tiles[i][0:128, 0:C], x_tiles[i][0:128, 0:C], ps_rep[:, 0:C]),
            reads=[x_tiles[i], ps_rep], writes=[xn_tiles[i]])
        p.do("s", lambda e, i=i: e.activation(sq[:, 0:C], xn_tiles[i][0:128, 0:C], AF.Square),
             reads=[xn_tiles[i]], writes=[sq])
        p.do("t", lambda e, i=i: e.matmul(
            ps_red[0:1, 0:C], cst["ones_sq"][0:128, 0:1], sq[:, 0:C],
            start=(i == 0), stop=(i == nd - 1)),
            reads=[sq], writes=[ps_red])
    p.do("v", lambda e: e.tensor_scalar(
        rstd_r[0:1, 0:C], ps_red[0:1, 0:C], 1.0 / D, 1e-5, ALU.mult, ALU.add),
        reads=[ps_red], writes=[rstd_r])
    p.do("s", lambda e: e.activation(rstd_r[0:1, 0:C], rstd_r[0:1, 0:C], AF.Sqrt),
         reads=[rstd_r], writes=[rstd_r])
    p.do("v", lambda e: e.reciprocal(rstd_r[0:1, 0:C], rstd_r[0:1, 0:C]),
         reads=[rstd_r], writes=[rstd_r])
    p.do("v", lambda e: e.tensor_copy(rowb[0:1, 0:C], rstd_r[0:1, 0:C]),
         reads=[rstd_r], writes=[rowb])
    p.do("t", lambda e: e.matmul(ps_rep[:, 0:C], cst["ones_sq"][0:1, 0:128],
                                 rowb[0:1, 0:C], start=True, stop=True),
         reads=[rowb], writes=[ps_rep])
    for i in range(nd):
        p.do("v", lambda e, i=i: e.tensor_mul(
            xn_tiles[i][0:128, 0:C], xn_tiles[i][0:128, 0:C], ps_rep[:, 0:C]),
            reads=[ps_rep], writes=[xn_tiles[i]])


def ln_scratch(p):
    return {
        "xb": p.sbuf("ln_xb", [128, 512], BF16),
        "sq": p.sbuf("ln_sq", [128, 512], BF16),
        "mean": p.sbuf("ln_mean", [1, 512], F32),
        "rstd": p.sbuf("ln_rstd", [1, 512], F32),
        "rowb": p.sbuf("ln_rowb", [1, 512], BF16),
    }


def load_weight_bf16(p, t, shape, name):
    k, m = shape
    assert k % 128 == 0
    tiles = []
    for i in range(k // 128):
        b = p.sbuf(f"w_{name}_{i}", [128, m], BF16)
        p.dma("g", b[:], t[i * 128:(i + 1) * 128, :], writes=[b])
        tiles.append(b)
    return tiles


def load_param_f32(p, t, shape2d, name):
    k, cols = shape2d
    assert k % 128 == 0
    tiles = []
    for i in range(k // 128):
        b = p.sbuf(f"p_{name}_{i}", [128, cols], F32)
        p.dma("q", b[:], t[i * 128:(i + 1) * 128, :], writes=[b])
        tiles.append(b)
    return tiles


def g_handles(nc):
    return dict(
        wr=nc.dram_tensor("g_wr", [DM, DM], F32, kind="ExternalInput"),
        wk=nc.dram_tensor("g_wk", [DM, DM], F32, kind="ExternalInput"),
        wv=nc.dram_tensor("g_wv", [DM, DM], F32, kind="ExternalInput"),
        mu=nc.dram_tensor("g_mu", [DM, 2], F32, kind="ExternalInput"))


def emit_g(p, cst, T_loc, x_in, x_out, gh):
    wr = load_weight_bf16(p, gh["wr"], (DM, DM), "g_wr")
    wk = load_weight_bf16(p, gh["wk"], (DM, DM), "g_wk")
    wv = load_weight_bf16(p, gh["wv"], (DM, DM), "g_wv")
    mu = load_param_f32(p, gh["mu"], (DM, 2), "g_mu")
    emit_g_sublayer(p, cst, T_loc, x_in, x_out, wr, wk, wv, mu)


def build_g_program(T_loc, with_store=True):
    p = Prog()
    nc = p.nc
    cst = make_consts(p)
    x_in = nc.dram_tensor("x_in", [DM, T_loc], F32, kind="ExternalInput")
    x_out = nc.dram_tensor("x_out", [DM, T_loc], F32, kind="ExternalOutput")
    emit_g(p, cst, T_loc, x_in, x_out, g_handles(nc))
    out_tok = p.count["g"]
    return p.finalize(final_waits=[("g", out_tok)])


def emit_g_sublayer(p, cst, T_loc, x_in, x_out, wr, wk, wv, mu):
    scr = ln_scratch(p)
    ps_red = p.psum("ps_red", [128, 512])
    ps_rep = p.psum("ps_rep", [128, 512])
    ps_a = p.psum("ps_a", [128, 512])
    ps_b = p.psum("ps_b", [128, 512])
    x_t = [p.sbuf(f"g_x{i}", [128, 512], F32) for i in range(NDM)]
    xn_t = [p.sbuf(f"g_xn{i}", [128, 513], BF16) for i in range(NDM)]
    xr_t = [p.sbuf(f"g_xr{i}", [128, 512], BF16) for i in range(NDM)]
    xk_t = [p.sbuf(f"g_xk{i}", [128, 512], BF16) for i in range(NDM)]
    tmp = p.sbuf("g_tmp", [128, 512], F32)
    r_t = [p.sbuf(f"g_r{i}", [128, 512], F32) for i in range(NDM)]
    ka_t = [p.sbuf(f"g_ka{i}", [128, 512], BF16) for i in range(NDM)]
    om_mu = p.sbuf("g_ommu", [128, 2 * NDM], F32)

    for i in range(NDM):
        p.do("v", lambda e, i=i: e.tensor_scalar(
            om_mu[:, 2 * i:2 * i + 2], mu[i][0:128, 0:2], -1.0, 1.0, ALU.mult, ALU.add),
            reads=[mu[i]], writes=[om_mu])
        p.do("v", lambda e, i=i: e.memset(xn_t[i][0:128, 0:1], 0.0),
             writes=[xn_t[i]])

    for ci, (off, C) in enumerate(chunk_plan(T_loc)):
        _g_chunk(p, cst, off, C, x_in, x_out, wr, wk, wv, mu, scr,
                 ps_red, ps_rep, ps_a, ps_b, x_t, xn_t, xr_t, xk_t, tmp,
                 r_t, ka_t, om_mu)


def _g_chunk(p, cst, off, C, x_in, x_out, wr, wk, wv, mu, scr,
             ps_red, ps_rep, ps_a, ps_b, x_t, xn_t, xr_t, xk_t, tmp,
             r_t, ka_t, om_mu):
        for i in range(NDM):
            p.dma("q", x_t[i][:, 0:C], x_in[i * 128:(i + 1) * 128, off:off + C],
                  writes=[x_t[i]])
        emit_ln_shifted(p, cst, C, x_t, xn_t, scr, (ps_red, ps_rep))
        for i in range(NDM):
            for (dst, mc) in ((xr_t, 0), (xk_t, 1)):
                p.do("v", lambda e, i=i, mc=mc: e.tensor_scalar_mul(
                    tmp[:, 0:C], xn_t[i][0:128, 0:C], om_mu[:, 2 * i + mc:2 * i + mc + 1]),
                    reads=[xn_t[i], om_mu], writes=[tmp])
                p.do("v", lambda e, i=i, dst=dst, mc=mc: e.scalar_tensor_tensor(
                    dst[i][0:128, 0:C], xn_t[i][0:128, 1:C + 1],
                    mu[i][0:128, mc:mc + 1], tmp[:, 0:C], ALU.mult, ALU.add),
                    reads=[xn_t[i], mu[i], tmp], writes=[dst[i]])
        for i in range(NDM):
            p.do("v", lambda e, i=i: e.tensor_copy(
                xn_t[i][0:128, 0:1], xn_t[i][0:128, C:C + 1]),
                reads=[xn_t[i]], writes=[xn_t[i]])
        for m in range(NDM):
            psm = ps_a if m % 2 == 0 else ps_b
            p.do_group("t", [
                (lambda e, m=m, k=k, psm=psm: e.matmul(
                    psm[:, 0:C], wr[k][0:128, m * 128:(m + 1) * 128],
                    xr_t[k][0:128, 0:C], start=(k == 0), stop=(k == NDM - 1)))
                for k in range(NDM)],
                reads=wr + xr_t, writes=[psm])
            p.do("s", lambda e, m=m, psm=psm: e.activation(
                r_t[m][0:128, 0:C], psm[:, 0:C], AF.Sigmoid),
                reads=[psm], writes=[r_t[m]])
        for m in range(NDM):
            psm = ps_a if m % 2 == 0 else ps_b
            p.do_group("t", [
                (lambda e, m=m, k=k, psm=psm: e.matmul(
                    psm[:, 0:C], wk[k][0:128, m * 128:(m + 1) * 128],
                    xk_t[k][0:128, 0:C], start=(k == 0), stop=(k == NDM - 1)))
                for k in range(NDM)],
                reads=wk + xk_t, writes=[psm])
            p.do("s", lambda e, m=m, psm=psm: e.activation(
                tmp[:, 0:C], psm[:, 0:C], AF.Relu),
                reads=[psm], writes=[tmp])
            p.do("s", lambda e, m=m: e.activation(
                ka_t[m][0:128, 0:C], tmp[:, 0:C], AF.Square),
                reads=[tmp], writes=[ka_t[m]])
        for m in range(NDM):
            psm = ps_a if m % 2 == 0 else ps_b
            p.do_group("t", [
                (lambda e, m=m, k=k, psm=psm: e.matmul(
                    psm[:, 0:C], wv[k][0:128, m * 128:(m + 1) * 128],
                    ka_t[k][0:128, 0:C], start=(k == 0), stop=(k == NDM - 1)))
                for k in range(NDM)],
                reads=wv + ka_t, writes=[psm])
            p.do("v", lambda e, m=m, psm=psm: e.scalar_tensor_tensor(
                tmp[:, 0:C], r_t[m][0:128, 0:C], 0.5, psm[:, 0:C],
                ALU.mult, ALU.mult),
                reads=[r_t[m], psm], writes=[tmp])
            p.do("v", lambda e, m=m: e.tensor_add(
                x_t[m][0:128, 0:C], x_t[m][0:128, 0:C], tmp[:, 0:C]),
                reads=[tmp], writes=[x_t[m]])
            p.dma("g", x_out[m * 128:(m + 1) * 128, off:off + C],
                  x_t[m][:, 0:C], reads=[x_t[m]])


def emit_ln_shifted(p, cst, C, x_tiles, xn_tiles, scr, ps):
    shifted = [_ColShift(b, 1) for b in xn_tiles]
    emit_ln(p, cst, C, x_tiles, shifted, nd=len(x_tiles), ps=ps, scr=scr)


class _ColShift:

    def __init__(self, buf, ofs):
        self._b = buf
        self._o = ofs

    def __getitem__(self, idx):
        ps, fs = idx
        return self._b.h[ps, fs.start + self._o: fs.stop + self._o]

    @property
    def w(self):
        return self._b.w

    @w.setter
    def w(self, v):
        self._b.w = v

    @property
    def readers(self):
        return self._b.readers

    @readers.setter
    def readers(self, v):
        self._b.readers = v


def shard_tokens(x_ld):
    L = x_ld.shape[0]
    T_own = L // NCORES
    out = []
    for k in range(NCORES):
        lo = k * T_own - HALO
        pad = max(0, -lo)
        sl = x_ld[max(0, lo):(k + 1) * T_own].T
        if pad:
            sl = np.concatenate([np.zeros((DM, pad), np.float32), sl], axis=1)
        out.append(np.ascontiguousarray(sl))
    return out


def unshard_tokens(parts):
    cols = [pp[:, HALO:] for pp in parts]
    return np.concatenate(cols, axis=1).T


class _RowOfs:

    def __init__(self, t, ofs):
        self._t = t
        self._o = ofs

    def __getitem__(self, idx):
        r, c = idx
        return self._t[r.start + self._o: r.stop + self._o, c]


class _SEView:

    def __init__(self, t, part):
        self._t = t
        self._p = part

    def __getitem__(self, idx):
        r, c = idx
        blk = r.start // DI
        x = r.start - blk * DI
        base = blk * 2 * DI + self._p * DI + x
        return self._t[base: base + (r.stop - r.start), c]


def build_mega(T_loc):
    p = Prog()
    nc = p.nc
    cst = make_consts(p)
    x_in = nc.dram_tensor("x_in", [DM, T_loc], F32, kind="ExternalInput")
    x_out = nc.dram_tensor("x_out", [DM, T_loc], F32, kind="ExternalOutput")
    x1 = nc.dram_tensor("x1_scr", [DM, T_loc], F32)
    x2 = nc.dram_tensor("x2_scr", [DM, T_loc], F32)
    z_hbm = nc.dram_tensor("z_scr", [DI, T_loc], BF16)
    u_hbm = nc.dram_tensor("u_scr", [DI, T_loc], BF16)
    cum_hbm = nc.dram_tensor("cum_scr", [DI, T_loc], BF16)
    yl_hbm = nc.dram_tensor("yl_scr", [DI, T_loc], BF16)
    pj_hbm = nc.dram_tensor("pj_scr", [128, T_loc], BF16)
    dt_hbm = nc.dram_tensor("dt_scr", [DI, T_loc], BF16)

    fh = f_handles(nc)
    gh = g_handles(nc)
    mh = {v: m_handles(nc, v, T_loc) for v in ("m0", "m1")}
    se = {v: (nc.dram_tensor(v + "_SE_loc", [2 * DI, DS], F32),
              nc.dram_tensor(v + "_SE_all", [NCORES * 2 * DI, DS], F32))
          for v in ("m0", "m1")}

    def emit_m(var, xa, xb):
        cfg = m_cfg(var, T_loc - HALO)
        se_loc, se_all = se[var]
        with p.scope():
            emit_ma(p, cst, T_loc, cfg, xa, _RowOfs(se_loc, 0),
                    _RowOfs(se_loc, DI), z_hbm, u_hbm, cum_hbm, yl_hbm,
                    pj_hbm, dt_hbm, mh[var])
        p.collective(lambda e: e.collective_compute(
            "AllGather", ALU.bypass,
            replica_groups=[list(range(NCORES))],
            ins=[se_loc[:, :]], outs=[se_all[:, :]]))
        p.barrier()
        with p.scope():
            emit_mb(p, cst, T_loc, cfg, xa, xb, _SEView(se_all, 0),
                    _SEView(se_all, 1), z_hbm, u_hbm, cum_hbm, yl_hbm,
                    pj_hbm, mh[var])

    with p.scope():
        emit_f_sublayer(p, cst, T_loc, x_in, x1, fh)
    emit_m("m0", x1, x2)
    with p.scope():
        emit_f_sublayer(p, cst, T_loc, x2, x1, fh)
    with p.scope():
        emit_g(p, cst, T_loc, x1, x2, gh)
    emit_m("m1", x2, x1)
    with p.scope():
        emit_g(p, cst, T_loc, x1, x_out, gh)
    return p.finalize(final_waits=[("g", p.count["g"]), ("q", p.count["q"])])


_PROG_CACHE = {}


def _prog(kind, T_loc):
    key = (kind, T_loc)
    if key not in _PROG_CACHE:
        if kind == "mega":
            _PROG_CACHE[key] = build_mega(T_loc)
        elif kind == "f":
            _PROG_CACHE[key] = build_f_program(T_loc)
        elif kind == "g":
            _PROG_CACHE[key] = build_g_program(T_loc)
        elif kind in ("m0a", "m1a"):
            _PROG_CACHE[key] = build_ma_program(T_loc, kind[:2])
        else:
            _PROG_CACHE[key] = build_mb_program(T_loc, kind[:2])
    return _PROG_CACHE[key]


def _ext_input_names(nc):
    names = set()
    for alloc in nc.m.functions[0].allocations:
        try:
            if alloc.kind == "ExternalInput":
                for m in alloc.memorylocations:
                    names.add(m.name)
        except AttributeError:
            pass
    return names


def _launch(nc, ins_list):
    names = _ext_input_names(nc)
    ins_list = [{k: np.ascontiguousarray(v) for k, v in d.items() if k in names}
                for d in ins_list]
    res = run_bass_kernel_spmd(nc, ins_list, core_ids=list(range(NCORES)))
    return res.results


def _mk_sel_f():
    sel = np.zeros((NET, NET * 128), BF)
    for e in range(NET):
        sel[e, e * 128:(e + 1) * 128] = 1
    return sel


def _mk_sel_m(cfg):
    NPR = cfg["npr"]
    sel = np.zeros((NPR, 33 * 128), BF)
    for i in range(DS):
        sel[cfg["rB"] + i, i * 128:(i + 1) * 128] = 1
        sel[cfg["rC"] + i, (DS + i) * 128:(DS + i + 1) * 128] = 1
    sel[NPR - 1, 2 * DS * 128:(2 * DS + 1) * 128] = 1
    return sel


def kernel(**inputs):
    ii = {k: np.asarray(v) for k, v in inputs.items()}
    x_full = ii["L0_attn_x"][0]
    L = x_full.shape[0]
    T_own = L // NCORES
    T_loc = T_own + HALO
    consts = host_consts()

    Wg, Wu, Wd = ii["L0_ffn_Wg"], ii["L0_ffn_Wu"], ii["L0_ffn_Wd"]
    Sg, Su, Sd = ii["L0_ffn_Sg"], ii["L0_ffn_Su"], ii["L0_ffn_Sd"]
    wg = np.zeros((NET, DM, HE), np.float32)
    wu = np.zeros((NET, DM, HE), np.float32)
    wd = np.zeros((NET, HE, DM), np.float32)
    for e in range(NE):
        wg[e] = Wg[e].T
        wu[e] = Wu[e].T
        wd[e] = Wd[e].T
    for sh in range(NS):
        SgT, SuT, SdT = Sg[sh].T, Su[sh].T, Sd[sh].T
        for q in range(4):
            e = NE + sh * 4 + q
            wg[e] = SgT[:, q * HE:(q + 1) * HE]
            wu[e] = SuT[:, q * HE:(q + 1) * HE]
            wd[e] = SdT[q * HE:(q + 1) * HE, :]
    f_w = dict(
        f_wr=np.ascontiguousarray(ii["L0_ffn_Wr"].T),
        f_rbias=np.broadcast_to(ii["L0_ffn_router_bias"], (128, NE)).copy(),
        f_sel=_mk_sel_f(), f_wg=wg, f_wu=wu, f_wd=wd)

    g_w = dict(
        g_wr=np.ascontiguousarray(ii["L1_ffn_W_r"].T),
        g_wk=np.ascontiguousarray(ii["L1_ffn_W_k"].T),
        g_wv=np.ascontiguousarray(ii["L1_ffn_W_v"].T),
        g_mu=np.stack([ii["L1_ffn_mu_r"], ii["L1_ffn_mu_k"]], axis=1))

    def m_weights(var, W_in, conv_w, conv_b, W_x, A_log, Dp, W_out, dtw, dtb):
        cfg = m_cfg(var, T_own)
        d = {
            var + "_win": np.ascontiguousarray(W_in.T),
            var + "_convw": np.ascontiguousarray(conv_w.reshape(DI, 4)),
            var + "_convb": conv_b.reshape(DI, 1),
            var + "_xproj": np.ascontiguousarray(W_x.T),
            var + "_alog": A_log,
            var + "_selbc": _mk_sel_m(cfg),
            var + "b_dparam": Dp.reshape(DI, 1),
            var + "_wout": np.ascontiguousarray(W_out.T),
            var + "_dtb": dtb.reshape(DI, 1),
        }
        if var == "m0":
            d[var + "_dtw"] = np.ascontiguousarray(dtw.T)
        else:
            d[var + "_wdt"] = dtw.reshape(DI, 1)
        return d

    m0_w = m_weights("m0", ii["L0_attn_in_proj_w"], ii["L0_attn_conv_w"],
                     ii["L0_attn_conv_b"], ii["L0_attn_x_proj_w"],
                     ii["L0_attn_A_log"], ii["L0_attn_D"],
                     ii["L0_attn_out_proj_w"], ii["L0_attn_dt_proj_w"],
                     ii["L0_attn_dt_proj_b"])
    m1_w = m_weights("m1", ii["L1_attn_W_in"], ii["L1_attn_conv_w"],
                     ii["L1_attn_conv_b"], ii["L1_attn_W_x"],
                     ii["L1_attn_A_log"], ii["L1_attn_D_param"],
                     ii["L1_attn_W_out"], ii["L1_attn_W_dt"],
                     ii["L1_attn_b_dt"])

    def m_core(var, c):
        mask = np.ones((128, T_loc), BF)
        oh = np.zeros((128, NCORES), np.float32)
        if c == 0:
            mask[:, 0:HALO] = 0.0
        else:
            oh[:, c - 1] = 1.0
        return {var + "_mask": mask, var + "b_oh": oh}

    xs = shard_tokens(x_full)

    outs = _launch(_prog("mega", T_loc),
                   [dict(x_in=xs[c], **f_w, **g_w, **m0_w, **m1_w, **consts,
                         **m_core("m0", c), **m_core("m1", c))
                    for c in range(NCORES)])
    xs = [outs[c]["x_out"] for c in range(NCORES)]

    out = unshard_tokens(xs)
    return out[None, :, :].astype(np.float32)


EGRP = 2
NET = 40


def f_handles(nc, pre="f"):
    return dict(
        wr=nc.dram_tensor(pre + "_wr", [DM, NE], F32, kind="ExternalInput"),
        rbias=nc.dram_tensor(pre + "_rbias", [128, NE], F32, kind="ExternalInput"),
        sel=nc.dram_tensor(pre + "_sel", [NET, NET * 128], BF16, kind="ExternalInput"),
        wg=nc.dram_tensor(pre + "_wg", [NET, DM, HE], F32, kind="ExternalInput"),
        wu=nc.dram_tensor(pre + "_wu", [NET, DM, HE], F32, kind="ExternalInput"),
        wd=nc.dram_tensor(pre + "_wd", [NET, HE, DM], F32, kind="ExternalInput"))


def build_f_program(T_loc):
    p = Prog()
    nc = p.nc
    cst = make_consts(p)
    x_in = nc.dram_tensor("x_in", [DM, T_loc], F32, kind="ExternalInput")
    x_out = nc.dram_tensor("x_out", [DM, T_loc], F32, kind="ExternalOutput")
    emit_f_sublayer(p, cst, T_loc, x_in, x_out, f_handles(nc))
    return p.finalize(final_waits=[("g", p.count["g"]), ("q", p.count["q"])])


def emit_f_sublayer(p, cst, T_loc, x_in, x_out, fh, pre="f"):
    nc = p.nc
    chunks = chunk_plan(T_loc)
    scr = ln_scratch(p)
    ps_red = p.psum(pre + "_psred", [128, 512])
    ps_rep = p.psum(pre + "_psrep", [128, 512])
    ps_g2 = [p.psum(pre + "_psg", [128, 512]) for _ in range(2)]
    ps_u2 = [p.psum(pre + "_psu", [128, 512]) for _ in range(2)]
    ps_cw2 = [p.psum(pre + "_pscw", [128, 512]) for _ in range(2)]
    ps_cw = ps_cw2[0]
    ps_o = [ps_red, ps_rep]

    w_r = load_weight_bf16(p, fh["wr"], (DM, NE), pre + "_wr")
    rbias = p.sbuf(pre + "_rbias_sb", [128, NE], F32)
    p.dma("q", rbias[:], fh["rbias"][:, :], writes=[rbias])
    sel = p.sbuf(pre + "_sel_sb", [NET, NET * 128], BF16)
    p.dma("q", sel[:], fh["sel"][:, :], writes=[sel])

    t_wg, t_wu, t_wd = fh["wg"], fh["wu"], fh["wd"]

    xn_all = [p.sbuf(pre + f"_xn{i}", [128, T_loc], BF16) for i in range(NDM)]
    cw_em = p.sbuf(pre + "_cwem", [NET, T_loc], BF16)
    out_acc = [p.sbuf(pre + f"_oa{i}", [128, T_loc], F32) for i in range(NDM)]
    x_t = [p.sbuf(pre + f"_x{i}", [128, 512], F32) for i in range(NDM)]

    wg_all = p.sbuf(pre + "_wgall", [128, NDM * EGRP * HE], BF16)
    wu_all = p.sbuf(pre + "_wuall", [128, NDM * EGRP * HE], BF16)
    wd_all = p.sbuf(pre + "_wdall", [128, 2 * EGRP * DM], BF16)
    h_sb = [p.sbuf(pre + f"_h{j}", [128, EGRP * T_loc], BF16) for j in range(2)]

    lg_sb = p.sbuf(pre + "_lg", [NE, 512], F32)
    lgT = x_t[0]
    s_sb = x_t[1]
    ind = x_t[2]
    ee = x_t[3]
    ww = x_t[4]
    tmp = x_t[5]
    tmp2 = [x_t[5], x_t[6]]
    mx = p.sbuf(pre + "_mx", [128, 8], F32)
    rs = p.sbuf(pre + "_rs", [128, 8], F32)
    cwT = p.sbuf(pre + "_cwT", [128, 512], F32)

    for i in range(NDM):
        p.do("v", lambda e, i=i: e.memset(out_acc[i][0:128, 0:T_loc], 0.0),
             writes=[out_acc[i]])
    p.do("v", lambda e: e.memset(cw_em[32:NET, 0:T_loc], 1.0), writes=[cw_em])
    p.do("v", lambda e: e.memset(ps_cw[:, 0:512], 0.0), writes=[ps_cw])

    for off, C in chunks:
        _f_router_chunk(p, cst, off, C, x_in, x_t, xn_all, cw_em, w_r, rbias,
                        scr, ps_red, ps_rep, ps_cw, lg_sb, lgT, s_sb, mx, ind,
                        ee, ww, rs, cwT)

    for g0 in range(0, NET, EGRP):
        for ei in range(EGRP):
            p.dma("g", wg_all[:, ei * NDM * HE:(ei + 1) * NDM * HE].rearrange(
                      "k (j h) -> k j h", j=NDM),
                  t_wg[g0 + ei].rearrange("(j k) h -> k j h", j=NDM),
                  writes=[wg_all])
            p.dma("g", wu_all[:, ei * NDM * HE:(ei + 1) * NDM * HE].rearrange(
                      "k (j h) -> k j h", j=NDM),
                  t_wu[g0 + ei].rearrange("(j k) h -> k j h", j=NDM),
                  writes=[wu_all])
            p.dma("g", wd_all[:, ei * 2 * DM:(ei + 1) * 2 * DM].rearrange(
                      "k (j m) -> k j m", j=2),
                  t_wd[g0 + ei].rearrange("(j k) m -> k j m", j=2),
                  writes=[wd_all])
        for ei in range(EGRP):
            par = (g0 + ei) % 2
            for off, C in chunks:
                _f_expert_chunk(p, cst, ei, g0 + ei, off, C, T_loc, xn_all,
                                cw_em, sel, wg_all, wu_all, h_sb,
                                ps_g2[par], ps_u2[par], ps_cw2[par], tmp2[par])
        for off, C in chunks:
            for m in range(NDM):
                psm = ps_o[m % 2]
                p.do_group("t", [
                    (lambda e, off=off, C=C, psm=psm, ei=ei, kj=kj,
                            co=(ei * 2 + kj) * DM + m * 128: e.matmul(
                        psm[:, 0:C], wd_all[0:128, co:co + 128],
                        h_sb[kj][0:128, ei * T_loc + off: ei * T_loc + off + C],
                        start=(ei == 0 and kj == 0), stop=(ei == EGRP - 1 and kj == 1)))
                    for ei in range(EGRP) for kj in range(2)],
                    reads=[wd_all, h_sb[0], h_sb[1]], writes=[psm])
                p.do("v", lambda e, m=m, off=off, C=C, psm=psm: e.tensor_add(
                    out_acc[m][0:128, off:off + C], out_acc[m][0:128, off:off + C],
                    psm[:, 0:C]),
                    reads=[psm], writes=[out_acc[m]])

    for off, C in chunks:
        for m in range(NDM):
            p.dma("q", x_t[m][:, 0:C], x_in[m * 128:(m + 1) * 128, off:off + C],
                  writes=[x_t[m]])
            p.do("v", lambda e, m=m, off=off, C=C: e.scalar_tensor_tensor(
                x_t[m][0:128, 0:C], out_acc[m][0:128, off:off + C], 0.5,
                x_t[m][0:128, 0:C], ALU.mult, ALU.add),
                reads=[out_acc[m]], writes=[x_t[m]])
            p.dma("g", x_out[m * 128:(m + 1) * 128, off:off + C], x_t[m][:, 0:C],
                  reads=[x_t[m]])


def _f_router_chunk(p, cst, off, C, x_in, x_t, xn_all, cw_em, w_r, rbias, scr,
                    ps_red, ps_rep, ps_cw, lg_sb, lgT, s_sb, mx, ind, ee, ww,
                    rs, cwT):
    nb = (C + 127) // 128
    for i in range(NDM):
        p.dma("q", x_t[i][:, 0:C], x_in[i * 128:(i + 1) * 128, off:off + C],
              writes=[x_t[i]])
    xn_view = [_ColShift(xn_all[i], off) for i in range(NDM)]
    emit_ln(p, cst, C, x_t, xn_view, nd=NDM, ps=(ps_red, ps_rep), scr=scr)
    p.do_group("t", [
        (lambda e, k=k, off=off, C=C: e.matmul(
            ps_cw[0:NE, 0:C], w_r[k][0:128, 0:NE],
            xn_all[k][0:128, off:off + C], start=(k == 0), stop=(k == NDM - 1)))
        for k in range(NDM)],
        reads=w_r + xn_all, writes=[ps_cw])
    p.do("s", lambda e, C=C: e.copy(lg_sb[0:NE, 0:C], ps_cw[0:NE, 0:C]),
         reads=[ps_cw], writes=[lg_sb])
    for b in range(nb):
        bc = min(128, C - b * 128)
        p.do("t", lambda e, b=b, bc=bc: e.transpose(
            ps_cw[0:bc, b * NE:(b + 1) * NE], lg_sb[0:NE, b * 128:b * 128 + bc],
            cst["ident_f"][0:NE, 0:NE]),
            reads=[lg_sb, cst["ident_f"]], writes=[ps_cw])
    p.do("v", lambda e, nb=nb: e.tensor_copy(lgT[0:128, 0:nb * NE], ps_cw[:, 0:nb * NE]),
         reads=[ps_cw], writes=[lgT])
    p.do("v", lambda e, nb=nb: e.tensor_tensor(
        s_sb[0:128, 0:nb * NE], lgT[0:128, 0:nb * NE],
        rbias[:].unsqueeze(1).broadcast_to([128, nb, NE]),
        ALU.add), reads=[lgT, rbias], writes=[s_sb])
    for r in range(4):
        p.do("v", lambda e, nb=nb: e.tensor_reduce(
            mx[:, 0:nb],
            s_sb[0:128, 0:nb * NE].rearrange("p (b e) -> p b e", e=NE),
            AX.X, ALU.max),
            reads=[s_sb], writes=[mx])
        p.do("v", lambda e, nb=nb: e.tensor_tensor(
            ee[0:128, 0:nb * NE], s_sb[0:128, 0:nb * NE],
            mx[:, 0:nb].unsqueeze(2).broadcast_to([128, nb, NE]),
            ALU.is_ge), reads=[s_sb, mx], writes=[ee])
        p.do("v", lambda e, nb=nb: e.scalar_tensor_tensor(
            s_sb[0:128, 0:nb * NE], ee[0:128, 0:nb * NE], -1e30,
            s_sb[0:128, 0:nb * NE], ALU.mult, ALU.add),
            reads=[ee], writes=[s_sb])
    p.do("v", lambda e, nb=nb: e.tensor_scalar(
        ind[0:128, 0:nb * NE], s_sb[0:128, 0:nb * NE], -1e29, None, ALU.is_lt),
        reads=[s_sb], writes=[ind])
    p.do("v", lambda e, nb=nb: e.tensor_reduce(
        mx[:, 0:nb], lgT[0:128, 0:nb * NE].rearrange("p (b e) -> p b e", e=NE),
        AX.X, ALU.max), reads=[lgT], writes=[mx])
    p.do("v", lambda e, nb=nb: e.tensor_tensor(
        ee[0:128, 0:nb * NE], lgT[0:128, 0:nb * NE],
        mx[:, 0:nb].unsqueeze(2).broadcast_to([128, nb, NE]),
        ALU.subtract), reads=[lgT, mx], writes=[ee])
    p.do("s", lambda e, nb=nb: e.activation(ee[0:128, 0:nb * NE], ee[0:128, 0:nb * NE], AF.Exp),
         reads=[ee], writes=[ee])
    p.do("v", lambda e, nb=nb: e.tensor_mul(ww[0:128, 0:nb * NE], ee[0:128, 0:nb * NE],
                                            ind[0:128, 0:nb * NE]),
         reads=[ee, ind], writes=[ww])
    p.do("v", lambda e, nb=nb: e.tensor_reduce(
        rs[:, 0:nb], ww[0:128, 0:nb * NE].rearrange("p (b e) -> p b e", e=NE),
        AX.X, ALU.add), reads=[ww], writes=[rs])
    p.do("v", lambda e, nb=nb: e.reciprocal(rs[:, 4:4 + nb], rs[:, 0:nb]),
         reads=[rs], writes=[rs])
    p.do("v", lambda e, nb=nb: e.tensor_tensor(
        cwT[0:128, 0:nb * NE], ww[0:128, 0:nb * NE],
        rs[:, 4:4 + nb].unsqueeze(2).broadcast_to([128, nb, NE]),
        ALU.mult), reads=[ww, rs], writes=[cwT])
    for b in range(nb):
        bc = min(128, C - b * 128)
        p.do("t", lambda e, b=b, bc=bc: e.transpose(
            ps_cw[0:NE, b * 128:b * 128 + bc], cwT[0:bc, b * NE:(b + 1) * NE],
            cst["ident_f"][0:bc, 0:bc]),
            reads=[cwT, cst["ident_f"]], writes=[ps_cw])
    p.do("s", lambda e, off=off, C=C: e.copy(cw_em[0:NE, off:off + C], ps_cw[0:NE, 0:C]),
         reads=[ps_cw], writes=[cw_em])


def _f_expert_chunk(p, cst, ei, e_abs, off, C, T_loc, xn_all, cw_em, sel,
                    wg_all, wu_all, h_sb, ps_g, ps_u, ps_cw, tmp):
    p.do("t", lambda e: e.matmul(
        ps_cw[:, 0:C], sel[0:NET, e_abs * 128:(e_abs + 1) * 128],
        cw_em[0:NET, off:off + C], start=True, stop=True),
        reads=[sel, cw_em], writes=[ps_cw])
    for hj in range(2):
        p.do_group("t", [
            (lambda e, k=k, co=(ei * NDM + k) * HE + hj * 128: e.matmul(
                ps_g[:, 0:C], wg_all[0:128, co:co + 128],
                xn_all[k][0:128, off:off + C], start=(k == 0), stop=(k == NDM - 1)))
            for k in range(NDM)],
            reads=[wg_all] + xn_all, writes=[ps_g])
        p.do_group("t", [
            (lambda e, k=k, co=(ei * NDM + k) * HE + hj * 128: e.matmul(
                ps_u[:, 0:C], wu_all[0:128, co:co + 128],
                xn_all[k][0:128, off:off + C], start=(k == 0), stop=(k == NDM - 1)))
            for k in range(NDM)],
            reads=[wu_all] + xn_all, writes=[ps_u])
        p.do("s", lambda e: e.activation(tmp[0:128, 0:C], ps_g[:, 0:C], AF.Silu),
             reads=[ps_g], writes=[tmp])
        p.do("v", lambda e: e.tensor_mul(tmp[0:128, 0:C], tmp[0:128, 0:C], ps_u[:, 0:C]),
             reads=[ps_u], writes=[tmp])
        p.do("v", lambda e, hj=hj: e.tensor_mul(
            h_sb[hj][0:128, ei * T_loc + off: ei * T_loc + off + C],
            tmp[0:128, 0:C], ps_cw[:, 0:C]),
            reads=[tmp, ps_cw], writes=[h_sb[hj]])


def m_cfg(variant, T_own):
    if variant == "m0":
        return dict(var="m0", npr=96, rB=64, rC=80, pstart=3, spos=T_own + 2)
    return dict(var="m1", npr=33, rB=0, rC=16, pstart=7, spos=T_own + 6)


def m_handles(nc, var, T_loc):
    npr = m_cfg(var, 0)["npr"]
    d = dict(
        win=nc.dram_tensor(var + "_win", [DM, 2 * DI], F32, kind="ExternalInput"),
        xproj=nc.dram_tensor(var + "_xproj", [DI, npr], F32, kind="ExternalInput"),
        convw=nc.dram_tensor(var + "_convw", [DI, 4], F32, kind="ExternalInput"),
        convb=nc.dram_tensor(var + "_convb", [DI, 1], F32, kind="ExternalInput"),
        dtb=nc.dram_tensor(var + "_dtb", [DI, 1], F32, kind="ExternalInput"),
        alog=nc.dram_tensor(var + "_alog", [DI, DS], F32, kind="ExternalInput"),
        selbc=nc.dram_tensor(var + "_selbc", [npr, (2 * DS + 1) * 128], BF16,
                             kind="ExternalInput"),
        mask=nc.dram_tensor(var + "_mask", [128, T_loc], BF16,
                            kind="ExternalInput"),
        wout=nc.dram_tensor(var + "_wout", [DI, DM], F32, kind="ExternalInput"),
        dparam=nc.dram_tensor(var + "b_dparam", [DI, 1], F32,
                              kind="ExternalInput"),
        oh=nc.dram_tensor(var + "b_oh", [128, NCORES], F32,
                          kind="ExternalInput"))
    if var == "m0":
        d["dtw"] = nc.dram_tensor(var + "_dtw", [DTR, DI], F32,
                                  kind="ExternalInput")
    else:
        d["dtw"] = nc.dram_tensor(var + "_wdt", [DI, 1], F32,
                                  kind="ExternalInput")
    return d


def build_ma_program(T_loc, variant):
    p = Prog()
    nc = p.nc
    cfg = m_cfg(variant, T_loc - HALO)
    cst = make_consts(p)
    x_in = nc.dram_tensor("x_in", [DM, T_loc], F32, kind="ExternalInput")
    S_out = nc.dram_tensor("S_out", [DI, DS], F32, kind="ExternalOutput")
    E_out = nc.dram_tensor("E_out", [DI, DS], F32, kind="ExternalOutput")
    z_hbm = nc.dram_tensor("z_scr", [DI, T_loc], BF16, kind="ExternalOutput")
    u_hbm = nc.dram_tensor("u_scr", [DI, T_loc], BF16, kind="ExternalOutput")
    cum_hbm = nc.dram_tensor("cum_scr", [DI, T_loc], BF16, kind="ExternalOutput")
    yl_hbm = nc.dram_tensor("yl_scr", [DI, T_loc], BF16, kind="ExternalOutput")
    pj_hbm = nc.dram_tensor("pj_scr", [128, T_loc], BF16, kind="ExternalOutput")
    dt_hbm = nc.dram_tensor("dt_scr", [DI, T_loc], BF16, kind="ExternalOutput")
    emit_ma(p, cst, T_loc, cfg, x_in, S_out, E_out, z_hbm, u_hbm, cum_hbm,
            yl_hbm, pj_hbm, dt_hbm, m_handles(nc, variant, T_loc))
    return p.finalize(final_waits=[("g", p.count["g"]), ("q", p.count["q"])])


def emit_ma(p, cst, T_loc, cfg, x_in, S_out, E_out, z_hbm, u_hbm, cum_hbm,
            yl_hbm, pj_hbm, dt_hbm, mh):
    nc = p.nc
    var = cfg["var"]
    NPR = cfg["npr"]
    pstart, spos = cfg["pstart"], cfg["spos"]
    WF = T_loc - pstart
    chunks = chunk_plan(T_loc)
    scr = ln_scratch(p)
    ps_big = p.psum(var + "_psbig", [128, 2560])
    ps_s = [p.psum(var + f"_pss{i}", [128, 512]) for i in range(3)]
    bank = [Buf(ps_big.h) for _ in range(2)]
    class _BankView(Buf):
        def __init__(self, h, ofs):
            super().__init__(h)
            self._o = ofs
        def __getitem__(self, idx):
            ps, fs = idx
            return self.h[ps, self._o + fs.start:self._o + fs.stop]
    bank = [_BankView(ps_big.h, 0), _BankView(ps_big.h, 512)]

    t_win = mh["win"]
    win2 = [p.sbuf(var + f"_win2{i}", [128, NDM * 512], BF16) for i in range(2)]
    xp_sb = [p.sbuf(var + f"_xp{j}", [128, NPR], BF16) for j in range(NDI)]
    t_xp = mh["xproj"]
    for j in range(NDI):
        p.dma("g", xp_sb[j][:], t_xp[j * 128:(j + 1) * 128, :], writes=[xp_sb[j]])
    convw = load_param_f32(p, mh["convw"], (DI, 4), var + "_convw")
    convb = load_param_f32(p, mh["convb"], (DI, 1), var + "_convb")
    dtb = load_param_f32(p, mh["dtb"], (DI, 1), var + "_dtb")
    alog = load_param_f32(p, mh["alog"], (DI, DS), var + "_alog")
    if var == "m0":
        dtw = p.sbuf(var + "_dtw_sb", [DTR, DI], BF16)
        p.dma("g", dtw[:], mh["dtw"][:, :], writes=[dtw])
    else:
        dtw = load_param_f32(p, mh["dtw"], (DI, 1), var + "_wdt")
    selbc = p.sbuf(var + "_selbc_sb", [NPR, (2 * DS + 1) * 128], BF16)
    p.dma("q", selbc[:], mh["selbc"][:, :], writes=[selbc])
    mask_sb = p.sbuf(var + "_mask_sb", [128, T_loc], BF16)
    p.dma("q", mask_sb[:], mh["mask"][:, :], writes=[mask_sb])

    A_sb = [p.sbuf(var + f"_A{j}", [128, DS], F32) for j in range(NDI)]
    for j in range(NDI):
        p.do("s", lambda e, j=j: e.activation(A_sb[j][0:128, 0:DS], alog[j][0:128, 0:DS], AF.Exp),
             reads=[alog[j]], writes=[A_sb[j]])
        p.do("v", lambda e, j=j: e.tensor_scalar_mul(A_sb[j][0:128, 0:DS], A_sb[j][0:128, 0:DS], -1.0),
             reads=[], writes=[A_sb[j]])

    x_t = [p.sbuf(var + f"_x{i}", [128, 512], F32) for i in range(NDM)]
    xn_t = [p.sbuf(var + f"_xn{i}", [128, 512], BF16) for i in range(NDM)]
    xs_t = [p.sbuf(var + f"_xs{j}", [128, 515], BF16) for j in range(NDI)]
    ub_t = [p.sbuf(var + f"_ub{j}", [128, 512], BF16) for j in range(NDI)]
    cum_col = [p.sbuf(var + f"_cc{j}", [128, 1], F32) for j in range(NDI)]
    projb = p.sbuf(var + "_projb", [NPR, 512], BF16)
    zb = [p.sbuf(var + f"_zb{i}", [128, 512], BF16) for i in range(2)]
    cacc = [p.sbuf(var + f"_cacc{i}", [128, 512], F32) for i in range(2)]
    dtraw1 = p.sbuf(var + "_dtraw", [128, 512], F32)
    dtraw = [dtraw1, dtraw1]
    sp11 = p.sbuf(var + "_sp1", [128, 512], F32)
    sp1 = [sp11, sp11]
    dt_f = [p.sbuf(var + f"_dtf{i}", [128, 512], F32) for i in range(2)]
    dt_b = [p.sbuf(var + f"_dtb2{i}", [128, 512], BF16) for i in range(2)]
    cumf1 = p.sbuf(var + "_cumf", [128, 512], F32)
    cumf = [cumf1, cumf1]
    cumb = [p.sbuf(var + f"_cumb{i}", [128, 512], BF16) for i in range(2)]
    E_sb = [p.sbuf(var + f"_E{j}", [128, DS], F32) for j in range(NDI)]

    cst["zeros512"] = p.sbuf(var + "_zeros", [128, 512], F32)
    p.do("v", lambda e: e.memset(cst["zeros512"][0:128, 0:512], 0.0),
         writes=[cst["zeros512"]])
    for j in range(NDI):
        p.do("v", lambda e, j=j: e.memset(cum_col[j][0:128, 0:1], 0.0), writes=[cum_col[j]])
        p.do("v", lambda e, j=j: e.memset(xs_t[j][0:128, 0:3], 0.0), writes=[xs_t[j]])

    for off, C in chunks:
        _ma1_chunk(p, cst, cfg, off, C, x_in, z_hbm, u_hbm, cum_hbm, pj_hbm,
                   dt_hbm, (t_win, win2), xp_sb, convw, convb, dtb, dtw, selbc,
                   mask_sb, A_sb, x_t, xn_t, xs_t, ub_t, cum_col, projb, zb,
                   cacc, dtraw, sp1, dt_f, dt_b, cumf, cumb, E_sb, scr,
                   ps_s, bank)
    p.barrier()

    JG = 4
    H0 = 512
    halves = []
    _o = 0
    while _o < WF:
        halves.append((_o, min(H0, WF - _o)))
        _o += min(H0, WF - _o)
    dtg = [p.sbuf(var + f"_dtg{i}", [128, WF], BF16) for i in range(JG)]
    ug2 = [p.sbuf(var + f"_ug{i}", [128, WF], BF16) for i in range(2)]
    ug = [ug2[0], ug2[1], ug2[0], ug2[1]]
    wg = [p.sbuf(var + f"_wg2{i}", [128, WF], BF16) for i in range(JG)]
    brep = [p.sbuf(var + f"_brep2{i}", [128, H0], BF16) for i in range(2)]
    crep = [p.sbuf(var + f"_crep2{i}", [128, H0], BF16) for i in range(2)]
    pjf = p.sbuf(var + "_pjf", [NPR, WF], BF16)
    dA4 = [p.sbuf(var + f"_dA2{i}", [128, H0], F32) for i in range(4)]
    dA = dA4
    in1 = [p.sbuf(var + f"_in12{i}", [128, H0], BF16) for i in range(4)]
    h_t = [p.sbuf(var + f"_h2{i}", [128, H0], BF16) for i in range(4)]
    yp = [p.sbuf(var + f"_yp2{i}", [128, H0], BF16) for i in range(4)]
    ylb = [p.sbuf(var + f"_ylb2{i}", [128, H0], BF16) for i in range(JG)]
    S_sb = [p.sbuf(var + f"_S{j}", [128, DS], F32) for j in range(NDI)]
    state = [p.sbuf(var + f"_st{j}", [128, DS], F32) for j in range(NDI)]

    class _YView(Buf):
        def __init__(self, h, ofs):
            super().__init__(h)
            self._o = ofs
        def __getitem__(self, idx):
            ps, fs = idx
            return self.h[ps, self._o + fs.start:self._o + fs.stop]
    ps_yv = [_YView(ps_big.h, 512 * i) for i in range(4)]
    ps_bc2 = _YView(ps_big.h, 2048)

    p.dma("q", pjf[0:NPR, 0:WF], pj_hbm[0:NPR, pstart:T_loc], writes=[pjf])
    for j0 in range(0, NDI, JG):
        for i in range(JG):
            j = j0 + i
            p.dma("q", dtg[i][:, 0:WF], dt_hbm[j * 128:(j + 1) * 128, pstart:T_loc],
                  writes=[dtg[i]])
            p.dma("q", ug[i][:, 0:WF], u_hbm[j * 128:(j + 1) * 128, pstart:T_loc],
                  writes=[ug[i]])
            p.do("v", lambda e, i=i: e.tensor_mul(wg[i][0:128, 0:WF],
                                                  dtg[i][0:128, 0:WF],
                                                  ug[i][0:128, 0:WF]),
                 reads=[dtg[i], ug[i]], writes=[wg[i]])
        for (hof, hw) in halves:
            for n in range(DS):
                bp = n % 2
                psB = ps_s[bp]
                psC = ps_s[2] if bp == 0 else ps_bc2
                p.do("t", lambda e, n=n, hof=hof, hw=hw, psB=psB: e.matmul(
                    psB[0:128, 0:hw], selbc[0:NPR, n * 128:(n + 1) * 128],
                    pjf[0:NPR, hof:hof + hw], start=True, stop=True),
                    reads=[selbc, pjf], writes=[psB])
                p.do("s", lambda e, hw=hw, bp=bp, psB=psB: e.copy(
                    brep[bp][0:128, 0:hw], psB[0:128, 0:hw]),
                    reads=[psB], writes=[brep[bp]])
                p.do("t", lambda e, n=n, hof=hof, hw=hw, psC=psC: e.matmul(
                    psC[0:128, 0:hw], selbc[0:NPR, (DS + n) * 128:(DS + n + 1) * 128],
                    pjf[0:NPR, hof:hof + hw], start=True, stop=True),
                    reads=[selbc, pjf], writes=[psC])
                p.do("s", lambda e, hw=hw, bp=bp, psC=psC: e.copy(
                    crep[bp][0:128, 0:hw], psC[0:128, 0:hw]),
                    reads=[psC], writes=[crep[bp]])
                for i in range(JG):
                    j = j0 + i
                    pa = i
                    p.do("s", lambda e, i=i, j=j, n=n, pa=pa, hof=hof, hw=hw: e.activation(
                        dA[pa][0:128, 0:hw], dtg[i][0:128, hof:hof + hw], AF.Exp,
                        bias=0.0, scale=A_sb[j][0:128, n:n + 1]),
                        reads=[dtg[i], A_sb[j]], writes=[dA[pa]])
                    p.do("v", lambda e, i=i, pa=pa, bp=bp, hof=hof, hw=hw: e.tensor_mul(
                        in1[pa][0:128, 0:hw], wg[i][0:128, hof:hof + hw],
                        brep[bp][0:128, 0:hw]),
                        reads=[wg[i], brep[bp]], writes=[in1[pa]])
                    if hof == 0:
                        p.do("v", lambda e, pa=pa, hw=hw: e.tensor_tensor_scan(
                            h_t[pa][0:128, 0:hw], dA[pa][0:128, 0:hw],
                            in1[pa][0:128, 0:hw], 0.0, ALU.mult, ALU.add),
                            reads=[dA[pa], in1[pa]], writes=[h_t[pa]])
                    else:
                        p.do("v", lambda e, j=j, n=n, pa=pa, hw=hw: e.tensor_tensor_scan(
                            h_t[pa][0:128, 0:hw], dA[pa][0:128, 0:hw],
                            in1[pa][0:128, 0:hw], state[j][0:128, n:n + 1],
                            ALU.mult, ALU.add),
                            reads=[dA[pa], in1[pa], state[j]], writes=[h_t[pa]])
                    if hof + hw < WF:
                        p.do("v", lambda e, j=j, n=n, pa=pa, hw=hw: e.tensor_copy(
                            state[j][0:128, n:n + 1], h_t[pa][0:128, hw - 1:hw]),
                            reads=[h_t[pa]], writes=[state[j]])
                    if hof <= spos - pstart < hof + hw:
                        sc = spos - pstart - hof
                        p.do("v", lambda e, j=j, n=n, pa=pa, sc=sc: e.tensor_copy(
                            S_sb[j][0:128, n:n + 1], h_t[pa][0:128, sc:sc + 1]),
                            reads=[h_t[pa]], writes=[S_sb[j]])
                    p.do("v", lambda e, pa=pa, bp=bp, hw=hw: e.tensor_mul(
                        yp[pa][0:128, 0:hw], h_t[pa][0:128, 0:hw], crep[bp][0:128, 0:hw]),
                        reads=[h_t[pa], crep[bp]], writes=[yp[pa]])
                    p.do("t", lambda e, n=n, i=i, pa=pa, hw=hw: e.matmul(
                        ps_yv[i].h[0:128, ps_yv[i]._o:ps_yv[i]._o + hw],
                        cst["ident_bf"][0:128, 0:128], yp[pa][0:128, 0:hw],
                        start=(n == 0), stop=(n == DS - 1)),
                        reads=[cst["ident_bf"], yp[pa]], writes=[ps_yv[i]])
            for i in range(JG):
                j = j0 + i
                p.do("s", lambda e, i=i, hw=hw: e.copy(
                    ylb[i][0:128, 0:hw],
                    ps_yv[i].h[0:128, ps_yv[i]._o:ps_yv[i]._o + hw]),
                    reads=[ps_yv[i]], writes=[ylb[i]])
                p.dma("g", yl_hbm[j * 128:(j + 1) * 128, pstart + hof:pstart + hof + hw],
                      ylb[i][:, 0:hw], reads=[ylb[i]],
                      **({"allow_slow_non_contiguous": True} if hw < 8 else {}))
    for j in range(NDI):
        p.dma("g", S_out[j * 128:(j + 1) * 128, :], S_sb[j][:, 0:DS], reads=[S_sb[j]])
        p.dma("g", E_out[j * 128:(j + 1) * 128, :], E_sb[j][:, 0:DS], reads=[E_sb[j]])


def _ma1_chunk(p, cst, cfg, off, C, x_in, z_hbm, u_hbm, cum_hbm, pj_hbm,
               dt_hbm, win_pack, xp_sb, convw, convb, dtb, dtw, selbc, mask_sb,
               A_sb, x_t, xn_t, xs_t, ub_t, cum_col, projb, zb, cacc, dtraw,
               sp1, dt_f, dt_b, cumf, cumb, E_sb, scr, ps_s, bank):
    var, NPR = cfg["var"], cfg["npr"]
    pstart, spos = cfg["pstart"], cfg["spos"]
    c0 = max(off, pstart)
    W = off + C - c0
    s0 = c0 - off
    for i in range(NDM):
        p.dma("q", x_t[i][:, 0:C], x_in[i * 128:(i + 1) * 128, off:off + C],
              writes=[x_t[i]])
    emit_ln(p, cst, C, x_t, xn_t, nd=NDM, ps=(ps_s[0], ps_s[1]), scr=scr)
    t_win, win2 = win_pack
    for mq in range(8):
        wb = win2[mq % 2]
        p.dma("g", wb[:, :].rearrange("k (j m) -> k j m", j=NDM),
              t_win[:, mq * 512:(mq + 1) * 512].rearrange("(j k) m -> k j m", j=NDM),
              writes=[wb])
        for mi in range(4):
            m = mq * 4 + mi
            psm = bank[m % 2]
            p.do_group("t", [
                (lambda e, m=m, mi=mi, k=k, psm=psm, C=C, wb=wb: e.matmul(
                    psm[0:128, 0:C], wb[0:128, k * 512 + mi * 128:k * 512 + (mi + 1) * 128],
                    xn_t[k][0:128, 0:C], start=(k == 0), stop=(k == NDM - 1)))
                for k in range(NDM)],
                reads=[wb] + xn_t, writes=[psm])
            if m < NDI:
                p.do("s", lambda e, m=m, psm=psm, C=C: e.copy(
                    xs_t[m][0:128, 3:3 + C], psm[0:128, 0:C]),
                    reads=[psm], writes=[xs_t[m]])
            else:
                zpar = zb[m % 2]
                p.do("s", lambda e, psm=psm, C=C, zpar=zpar: e.copy(
                    zpar[0:128, 0:C], psm[0:128, 0:C]),
                    reads=[psm], writes=[zpar])
                p.dma("g", z_hbm[(m - NDI) * 128:(m - NDI + 1) * 128, off:off + C],
                      zpar[:, 0:C], reads=[zpar])
    for j in range(NDI):
        pa = j % 2
        p.do("v", lambda e, j=j, C=C, pa=pa: e.tensor_scalar_mul(
            cacc[pa][0:128, 0:C], xs_t[j][0:128, 0:C], convw[j][0:128, 0:1]),
            reads=[xs_t[j], convw[j]], writes=[cacc[pa]])
        for tap in range(1, 4):
            p.do("v", lambda e, j=j, tap=tap, C=C, pa=pa: e.scalar_tensor_tensor(
                cacc[pa][0:128, 0:C], xs_t[j][0:128, tap:tap + C],
                convw[j][0:128, tap:tap + 1], cacc[pa][0:128, 0:C],
                ALU.mult, ALU.add),
                reads=[xs_t[j], convw[j]], writes=[cacc[pa]])
        p.do("s", lambda e, j=j, C=C, pa=pa: e.activation(
            ub_t[j][0:128, 0:C], cacc[pa][0:128, 0:C], AF.Silu,
            bias=convb[j][0:128, 0:1], scale=1.0),
            reads=[cacc[pa], convb[j]], writes=[ub_t[j]])
        p.dma("g", u_hbm[j * 128:(j + 1) * 128, off:off + C], ub_t[j][:, 0:C],
              reads=[ub_t[j]])
        p.do("v", lambda e, j=j, C=C: e.tensor_copy(
            xs_t[j][0:128, 0:3], xs_t[j][0:128, C:C + 3]),
            reads=[], writes=[xs_t[j]])
    p.do_group("t", [
        (lambda e, k=k, C=C: e.matmul(
            ps_s[2][0:NPR, 0:C], xp_sb[k][0:128, 0:NPR], ub_t[k][0:128, 0:C],
            start=(k == 0), stop=(k == NDI - 1)))
        for k in range(NDI)],
        reads=xp_sb + ub_t, writes=[ps_s[2]])
    p.do("s", lambda e, C=C: e.copy(projb[0:NPR, 0:C], ps_s[2][0:NPR, 0:C]),
         reads=[ps_s[2]], writes=[projb])
    p.dma("g", pj_hbm[0:NPR, off:off + C], projb[0:NPR, 0:C], reads=[projb])
    for j in range(NDI):
        pa = j % 2
        psd = bank[pa]
        if var == "m0":
            p.do("t", lambda e, j=j, C=C, psd=psd: e.matmul(
                psd[0:128, 0:C], dtw[0:DTR, j * 128:(j + 1) * 128],
                projb[0:DTR, 0:C], start=True, stop=True),
                reads=[dtw, projb], writes=[psd])
            p.do("v", lambda e, C=C, pa=pa, psd=psd: e.tensor_copy(
                dtraw[pa][0:128, 0:C], psd[0:128, 0:C]),
                reads=[psd], writes=[dtraw[pa]])
        else:
            p.do("t", lambda e, j=j, C=C, psd=psd: e.matmul(
                psd[0:128, 0:C], selbc[0:NPR, 2 * DS * 128:(2 * DS + 1) * 128],
                projb[0:NPR, 0:C], start=True, stop=True),
                reads=[selbc, projb], writes=[psd])
            p.do("v", lambda e, j=j, C=C, pa=pa, psd=psd: e.tensor_scalar_mul(
                dtraw[pa][0:128, 0:C], psd[0:128, 0:C], dtw[j][0:128, 0:1]),
                reads=[psd, dtw[j]], writes=[dtraw[pa]])
        p.do("s", lambda e, j=j, C=C, pa=pa: e.activation(
            sp1[pa][0:128, 0:C], dtraw[pa][0:128, 0:C], AF.Abs,
            bias=dtb[j][0:128, 0:1], scale=1.0),
            reads=[dtraw[pa], dtb[j]], writes=[sp1[pa]])
        p.do("s", lambda e, C=C, pa=pa: e.activation(
            sp1[pa][0:128, 0:C], sp1[pa][0:128, 0:C], AF.Exp,
            bias=0.0, scale=cst["negone"][0:128, 0:1]),
            reads=[sp1[pa], cst["negone"]], writes=[sp1[pa]])
        p.do("s", lambda e, C=C, pa=pa: e.activation(
            sp1[pa][0:128, 0:C], sp1[pa][0:128, 0:C], AF.Ln,
            bias=cst["one"][0:128, 0:1], scale=1.0),
            reads=[sp1[pa], cst["one"]], writes=[sp1[pa]])
        p.do("s", lambda e, j=j, C=C, pa=pa: e.activation(
            dt_f[pa][0:128, 0:C], dtraw[pa][0:128, 0:C], AF.Relu,
            bias=dtb[j][0:128, 0:1], scale=1.0),
            reads=[dtraw[pa], dtb[j]], writes=[dt_f[pa]])
        p.do("v", lambda e, C=C, pa=pa: e.tensor_add(
            dt_f[pa][0:128, 0:C], dt_f[pa][0:128, 0:C], sp1[pa][0:128, 0:C]),
            reads=[sp1[pa]], writes=[dt_f[pa]])
        p.do("v", lambda e, C=C, off=off, pa=pa: e.tensor_mul(
            dt_f[pa][0:128, 0:C], dt_f[pa][0:128, 0:C], mask_sb[0:128, off:off + C]),
            reads=[mask_sb], writes=[dt_f[pa]])
        p.do("v", lambda e, C=C, pa=pa: e.tensor_copy(
            dt_b[pa][0:128, 0:C], dt_f[pa][0:128, 0:C]),
            reads=[dt_f[pa]], writes=[dt_b[pa]])
        p.dma("g", dt_hbm[j * 128:(j + 1) * 128, off:off + C], dt_b[pa][:, 0:C],
              reads=[dt_b[pa]])
        p.do("v", lambda e, j=j, pa=pa, s0=s0, W=W: e.tensor_tensor_scan(
            cumf[pa][0:128, 0:W], dt_f[pa][0:128, s0:s0 + W],
            cst["zeros512"][0:128, 0:W],
            cum_col[j][0:128, 0:1], ALU.add, ALU.add),
            reads=[dt_f[pa], cst["zeros512"], cum_col[j]], writes=[cumf[pa]])
        p.do("v", lambda e, j=j, pa=pa, W=W: e.tensor_copy(
            cum_col[j][0:128, 0:1], cumf[pa][0:128, W - 1:W]),
            reads=[cumf[pa]], writes=[cum_col[j]])
        p.do("s", lambda e, pa=pa, W=W: e.copy(cumb[pa][0:128, 0:W], cumf[pa][0:128, 0:W]),
             reads=[cumf[pa]], writes=[cumb[pa]])
        p.dma("g", cum_hbm[j * 128:(j + 1) * 128, c0:c0 + W], cumb[pa][:, 0:W],
              reads=[cumb[pa]])
        if c0 <= spos < c0 + W:
            sc = spos - c0
            p.do("v", lambda e, j=j, sc=sc, pa=pa: e.tensor_mul(
                E_sb[j][0:128, 0:DS], A_sb[j][0:128, 0:DS],
                cumf[pa][0:128, sc:sc + 1].broadcast_to([128, DS])),
                reads=[A_sb[j], cumf[pa]], writes=[E_sb[j]])
            p.do("s", lambda e, j=j: e.activation(
                E_sb[j][0:128, 0:DS], E_sb[j][0:128, 0:DS], AF.Exp),
                reads=[], writes=[E_sb[j]])


def build_mb_program(T_loc, variant):
    p = Prog()
    nc = p.nc
    cfg = m_cfg(variant, T_loc - HALO)
    cst = make_consts(p)
    x_in = nc.dram_tensor("x_in", [DM, T_loc], F32, kind="ExternalInput")
    x_out = nc.dram_tensor("x_out", [DM, T_loc], F32, kind="ExternalOutput")
    S_all = nc.dram_tensor("S_all", [NCORES * DI, DS], F32, kind="ExternalInput")
    E_all = nc.dram_tensor("E_all", [NCORES * DI, DS], F32, kind="ExternalInput")
    z_hbm = nc.dram_tensor("z_scr", [DI, T_loc], BF16, kind="ExternalInput")
    u_hbm = nc.dram_tensor("u_scr", [DI, T_loc], BF16, kind="ExternalInput")
    cum_hbm = nc.dram_tensor("cum_scr", [DI, T_loc], BF16, kind="ExternalInput")
    yl_hbm = nc.dram_tensor("yl_scr", [DI, T_loc], BF16, kind="ExternalInput")
    pj_hbm = nc.dram_tensor("pj_scr", [128, T_loc], BF16, kind="ExternalInput")
    emit_mb(p, cst, T_loc, cfg, x_in, x_out, S_all, E_all, z_hbm, u_hbm,
            cum_hbm, yl_hbm, pj_hbm, m_handles(nc, variant, T_loc))
    return p.finalize(final_waits=[("g", p.count["g"]), ("q", p.count["q"])])


def emit_mb(p, cst, T_loc, cfg, x_in, x_out, S_all, E_all, z_hbm, u_hbm,
            cum_hbm, yl_hbm, pj_hbm, mh):
    nc = p.nc
    var, NPR = cfg["var"], cfg["npr"]
    pstart = cfg["pstart"]
    WF = T_loc - pstart
    H0 = 512
    halves = []
    _o = 0
    while _o < WF:
        halves.append((_o, min(H0, WF - _o)))
        _o += min(H0, WF - _o)
    JG = 4
    ps_big = p.psum(var + "b_psbig", [128, 2560])
    ps_s = [p.psum(var + f"b_pss{i}", [128, 512]) for i in range(3)]

    class _YView(Buf):
        def __init__(self, h, ofs):
            super().__init__(h)
            self._o = ofs
        def __getitem__(self, idx):
            ps, fs = idx
            return self.h[ps, self._o + fs.start:self._o + fs.stop]
    ps_yv = [_YView(ps_big.h, 512 * i) for i in range(4)]
    ps_c2 = _YView(ps_big.h, 2048)

    wout_sb = [p.sbuf(var + f"b_wo{j}", [128, DM], BF16) for j in range(NDI)]
    t_wout = mh["wout"]
    for j in range(NDI):
        p.dma("g", wout_sb[j][:], t_wout[j * 128:(j + 1) * 128, :], writes=[wout_sb[j]])
    alog = load_param_f32(p, mh["alog"], (DI, DS), var + "b_alog")
    dparam = load_param_f32(p, mh["dparam"], (DI, 1), var + "b_dparam")
    selbc = p.sbuf(var + "b_selbc_sb", [NPR, (2 * DS + 1) * 128], BF16)
    p.dma("q", selbc[:], mh["selbc"][:, :], writes=[selbc])
    oh = p.sbuf(var + "b_oh_sb", [128, NCORES], F32)
    p.dma("q", oh[:], mh["oh"][:, :], writes=[oh])

    A_sb = [p.sbuf(var + f"b_A{j}", [128, DS], F32) for j in range(NDI)]
    for j in range(NDI):
        p.do("s", lambda e, j=j: e.activation(A_sb[j][0:128, 0:DS], alog[j][0:128, 0:DS], AF.Exp),
             reads=[alog[j]], writes=[A_sb[j]])
        p.do("v", lambda e, j=j: e.tensor_scalar_mul(A_sb[j][0:128, 0:DS], A_sb[j][0:128, 0:DS], -1.0),
             reads=[], writes=[A_sb[j]])

    H_sb = [p.sbuf(var + f"b_H{j}", [128, DS], F32) for j in range(NDI)]
    Tc = p.sbuf(var + "b_Tc", [128, DS], F32)
    Sr = p.sbuf(var + "b_Sr", [128, DS], F32)
    Er = p.sbuf(var + "b_Er", [128, DS], F32)
    for j in range(NDI):
        p.do("v", lambda e, j=j: e.memset(H_sb[j][0:128, 0:DS], 0.0), writes=[H_sb[j]])
        for r in range(NCORES):
            p.dma("q", Sr[:, 0:DS], S_all[r * DI + j * 128: r * DI + (j + 1) * 128, :],
                  writes=[Sr])
            if r == 0:
                p.do("v", lambda e: e.tensor_copy(Tc[0:128, 0:DS], Sr[0:128, 0:DS]),
                     reads=[Sr], writes=[Tc])
            else:
                p.dma("q", Er[:, 0:DS], E_all[r * DI + j * 128: r * DI + (j + 1) * 128, :],
                      writes=[Er])
                p.do("v", lambda e: e.tensor_mul(Tc[0:128, 0:DS], Tc[0:128, 0:DS],
                                                 Er[0:128, 0:DS]),
                     reads=[Er], writes=[Tc])
                p.do("v", lambda e: e.tensor_add(Tc[0:128, 0:DS], Tc[0:128, 0:DS],
                                                 Sr[0:128, 0:DS]),
                     reads=[Sr], writes=[Tc])
            p.do("v", lambda e, j=j, r=r: e.scalar_tensor_tensor(
                H_sb[j][0:128, 0:DS], Tc[0:128, 0:DS], oh[:, r:r + 1],
                H_sb[j][0:128, 0:DS], ALU.mult, ALU.add),
                reads=[Tc, oh], writes=[H_sb[j]])

    pjf = p.sbuf(var + "b_pjf", [NPR, WF], BF16)
    p.dma("q", pjf[0:NPR, 0:WF], pj_hbm[0:NPR, pstart:T_loc], writes=[pjf])
    cum_g = [p.sbuf(var + f"b_cum{i}", [128, WF], BF16) for i in range(JG)]
    yl_g = [p.sbuf(var + f"b_yl{i}", [128, H0], BF16) for i in range(2)]
    u_g = [p.sbuf(var + f"b_u{i}", [128, H0], BF16) for i in range(2)]
    z_g = [p.sbuf(var + f"b_z{i}", [128, H0], BF16) for i in range(2)]
    zf = [p.sbuf(var + f"b_zf{i}", [128, H0], F32) for i in range(2)]
    crep = [p.sbuf(var + f"b_crep{i}", [128, H0], BF16) for i in range(2)]
    Pt2 = [p.sbuf(var + f"b_P{i}", [128, H0], F32) for i in range(4)]
    yp = [p.sbuf(var + f"b_yp{i}", [128, H0], BF16) for i in range(4)]
    yt = [p.sbuf(var + f"b_yt{i}", [128, H0], F32) for i in range(2)]
    yg_t = [p.sbuf(var + f"b_yg{j}", [128, WF], BF16) for j in range(NDI)]

    for j0 in range(0, NDI, JG):
        for i in range(JG):
            j = j0 + i
            p.dma("q", cum_g[i][:, 0:WF], cum_hbm[j * 128:(j + 1) * 128, pstart:T_loc],
                  writes=[cum_g[i]])

        for (hof, hw) in halves:
            for n in range(DS):
                bp = n % 2
                psC = ps_s[bp]
                p.do("t", lambda e, n=n, hof=hof, hw=hw, psC=psC: e.matmul(
                    psC[0:128, 0:hw],
                    selbc[0:NPR, (DS + n) * 128:(DS + n + 1) * 128],
                    pjf[0:NPR, hof:hof + hw], start=True, stop=True),
                    reads=[selbc, pjf], writes=[psC])
                p.do("s", lambda e, hw=hw, bp=bp, psC=psC: e.copy(
                    crep[bp][0:128, 0:hw], psC[0:128, 0:hw]),
                    reads=[psC], writes=[crep[bp]])
                for i in range(JG):
                    j = j0 + i
                    pa4 = i
                    p.do("s", lambda e, i=i, j=j, n=n, pa4=pa4, hof=hof, hw=hw: e.activation(
                        Pt2[pa4][0:128, 0:hw], cum_g[i][0:128, hof:hof + hw], AF.Exp,
                        bias=0.0, scale=A_sb[j][0:128, n:n + 1]),
                        reads=[cum_g[i], A_sb[j]], writes=[Pt2[pa4]])
                    p.do("v", lambda e, j=j, n=n, pa4=pa4, bp=bp, hw=hw: e.scalar_tensor_tensor(
                        yp[pa4][0:128, 0:hw], Pt2[pa4][0:128, 0:hw],
                        H_sb[j][0:128, n:n + 1], crep[bp][0:128, 0:hw],
                        ALU.mult, ALU.mult),
                        reads=[Pt2[pa4], H_sb[j], crep[bp]], writes=[yp[pa4]])
                    p.do("t", lambda e, n=n, i=i, pa4=pa4, hw=hw: e.matmul(
                        ps_yv[i].h[0:128, ps_yv[i]._o:ps_yv[i]._o + hw],
                        cst["ident_bf"][0:128, 0:128], yp[pa4][0:128, 0:hw],
                        start=(n == 0), stop=(n == DS - 1)),
                        reads=[cst["ident_bf"], yp[pa4]], writes=[ps_yv[i]])
            for i in range(JG):
                j = j0 + i
                ya = i % 2
                p.dma("g", z_g[ya][:, 0:hw],
                      z_hbm[j * 128:(j + 1) * 128, pstart + hof:pstart + hof + hw],
                      writes=[z_g[ya]],
                      **({"allow_slow_non_contiguous": True} if hw < 8 else {}))
                p.dma("g", u_g[ya][:, 0:hw],
                      u_hbm[j * 128:(j + 1) * 128, pstart + hof:pstart + hof + hw],
                      writes=[u_g[ya]],
                      **({"allow_slow_non_contiguous": True} if hw < 8 else {}))
                p.dma("q", yl_g[ya][:, 0:hw],
                      yl_hbm[j * 128:(j + 1) * 128, pstart + hof:pstart + hof + hw],
                      writes=[yl_g[ya]],
                      **({"allow_slow_non_contiguous": True} if hw < 8 else {}))
                p.do("s", lambda e, ya=ya, hw=hw: e.activation(
                    zf[ya][0:128, 0:hw], z_g[ya][0:128, 0:hw], AF.Silu),
                    reads=[z_g[ya]], writes=[zf[ya]])
                p.do("v", lambda e, i=i, ya=ya, hw=hw: e.tensor_add(
                    yt[ya][0:128, 0:hw], yl_g[ya][0:128, 0:hw],
                    ps_yv[i].h[0:128, ps_yv[i]._o:ps_yv[i]._o + hw]),
                    reads=[yl_g[ya], ps_yv[i]], writes=[yt[ya]])
                p.do("v", lambda e, j=j, ya=ya, hw=hw: e.scalar_tensor_tensor(
                    yt[ya][0:128, 0:hw], u_g[ya][0:128, 0:hw],
                    dparam[j][0:128, 0:1], yt[ya][0:128, 0:hw],
                    ALU.mult, ALU.add),
                    reads=[u_g[ya], dparam[j]], writes=[yt[ya]])
                p.do("v", lambda e, j=j, ya=ya, hof=hof, hw=hw: e.tensor_mul(
                    yg_t[j][0:128, hof:hof + hw], yt[ya][0:128, 0:hw],
                    zf[ya][0:128, 0:hw]),
                    reads=[yt[ya], zf[ya]], writes=[yg_t[j]])

    x_t = [p.sbuf(var + f"b_x{i}", [128, 512], F32) for i in range(NDM)]
    for off, C in chunk_plan(T_loc):
        c0 = max(off, pstart)
        W = off + C - c0
        s0 = c0 - off
        for i in range(NDM):
            p.dma("q", x_t[i][:, 0:C], x_in[i * 128:(i + 1) * 128, off:off + C],
                  writes=[x_t[i]])
        for m in range(NDM):
            psm = ps_s[m % 2]
            p.do_group("t", [
                (lambda e, m=m, k=k, W=W, c0=c0, psm=psm: e.matmul(
                    psm[:, 0:W], wout_sb[k][0:128, m * 128:(m + 1) * 128],
                    yg_t[k][0:128, c0 - pstart:c0 - pstart + W],
                    start=(k == 0), stop=(k == NDI - 1)))
                for k in range(NDI)],
                reads=wout_sb + yg_t, writes=[psm])
            p.do("v", lambda e, m=m, s0=s0, W=W, psm=psm: e.tensor_add(
                x_t[m][0:128, s0:s0 + W], x_t[m][0:128, s0:s0 + W], psm[:, 0:W]),
                reads=[psm], writes=[x_t[m]])
            p.dma("g", x_out[m * 128:(m + 1) * 128, off:off + C], x_t[m][:, 0:C],
                  reads=[x_t[m]])



# version 08
# speedup vs baseline: 1.1024x; 1.1024x over previous
import contextlib

import numpy as np
import ml_dtypes
import concourse.bass as bass
import concourse.mybir as mybir
from concourse.bass_utils import run_bass_kernel_spmd

F32 = mybir.dt.float32
BF16 = mybir.dt.bfloat16
AF = mybir.ActivationFunctionType
ALU = mybir.AluOpType
AX = mybir.AxisListType
BF = ml_dtypes.bfloat16

NCORES = 8
HALO = 8
DM = 1024
DI = 2048
DS = 16
DTR = 64
NE = 32
HE = 256
NS = 2
HS = 1024
NDM = DM // 128
NDI = DI // 128


class Buf:
    def __init__(self, h):
        self.h = h
        self.w = None
        self.readers = []

    def __getitem__(self, idx):
        return self.h[idx]


class Prog:
    def __init__(self):
        self.nc = bass.Bass("TRN2", target_bir_lowering=False)
        self.stack = contextlib.ExitStack()
        self.alloc_stack = self.stack
        self.ops = {k: [] for k in ("t", "v", "s", "gc", "g", "q")}
        self.count = {k: 0 for k in self.ops}
        self.sems = {}
        self.waited = {k: {} for k in self.ops}
        self._pending = {}
        self._names = {}
        self.dma_idx = {"g": 0, "q": 0}
        self.slot_uses = {}
        for k in self.ops:
            self._sem(k)

    def _uname(self, base):
        i = self._names.get(base, 0)
        self._names[base] = i + 1
        return base if i == 0 else f"{base}_{i}"

    def sbuf(self, base, shape, dtype):
        return Buf(self.alloc_stack.enter_context(
            self.nc.sbuf_tensor(self._uname(base), shape, dtype)))

    def psum(self, base, shape, dtype=F32):
        return Buf(self.alloc_stack.enter_context(
            self.nc.psum_tensor(self._uname(base), shape, dtype)))

    def dram(self, base, shape, dtype, kind="Internal"):
        return self.nc.dram_tensor(self._uname(base), shape, dtype, kind=kind)

    @contextlib.contextmanager
    def scope(self):
        prev = self.alloc_stack
        self.alloc_stack = contextlib.ExitStack()
        try:
            yield
        finally:
            self.barrier()
            self.alloc_stack.close()
            self.alloc_stack = prev

    def _sem(self, key):
        if key not in self.sems:
            self.sems[key] = self.stack.enter_context(
                self.nc.semaphore(f"sem_{key}"))
        return self.sems[key]

    ENG_STREAM = {"t": "t", "v": "v", "s": "s", "gc": "g", "g": "g", "q": "q"}
    ENG_STEP = {"t": 1, "v": 1, "s": 1, "gc": 1, "g": 16, "q": 16}
    NSLOT = 8

    def _flush_tokens(self, queue):
        return [(f"{queue}{i}", 16 * self.slot_uses[f"{queue}{i}"])
                for i in range(self.NSLOT)
                if self.slot_uses.get(f"{queue}{i}", 0) > 0]

    def raw_emit(self, eng, fn, deps=()):
        stream = self.ENG_STREAM[eng]
        deps = list(deps) + self._pending.pop(eng, [])
        if eng in ("g", "q"):
            i = self.dma_idx[eng]
            self.dma_idx[eng] = i + 1
            slot = f"{eng}{i % self.NSLOT}"
            sem = self._sem(slot)
            uses = self.slot_uses.get(slot, 0) + 1
            self.slot_uses[slot] = uses
            if uses > 1:
                deps.append((slot, 16 * (uses - 1)))
            tok = (slot, 16 * uses)
            step = 16
        else:
            sem = self._sem(eng)
            step = self.ENG_STEP[eng]
            self.count[eng] += step
            tok = (eng, self.count[eng])
        waits = []
        for d in deps:
            if d is None:
                continue
            dk, dv = d
            if self.waited[eng].get(dk, -1) >= dv:
                continue
            waits.append((dk, dv))
            self.waited[eng][dk] = dv

        def run(e, waits=tuple(waits), fn=fn, sem=sem, step=step):
            for dk, dv in waits:
                e.wait_ge(self._sem(dk), dv)
            fn(e).then_inc(sem, step)

        self.ops[stream].append(run)
        return tok

    def do(self, eng, fn, reads=(), writes=(), deps=()):
        d = list(deps)
        for b in reads:
            if b.w:
                d.append(b.w)
        for b in writes:
            d.extend(b.readers)
            if b.w:
                d.append(b.w)
        tok = self.raw_emit(eng, fn, deps=d)
        for b in reads:
            b.readers.append(tok)
        for b in writes:
            b.w = tok
            b.readers = []
        return tok

    def do_group(self, eng, fns, reads=(), writes=(), deps=()):
        assert eng not in ("g", "q")
        d = list(deps) + self._pending.pop(eng, [])
        for b in reads:
            if b.w:
                d.append(b.w)
        for b in writes:
            d.extend(b.readers)
            if b.w:
                d.append(b.w)
        sem = self._sem(eng)
        waits = []
        for dd in d:
            if dd is None:
                continue
            dk, dv = dd
            if self.waited[eng].get(dk, -1) >= dv:
                continue
            if dk == eng and dv > self.count[eng]:
                raise RuntimeError("self-dep on future op")
            waits.append((dk, dv))
            self.waited[eng][dk] = dv
        n = len(fns)

        for i, fn in enumerate(fns):
            def run(e, fn=fn, w=tuple(waits) if i == 0 else (), last=(i == n - 1),
                    sem=sem):
                for dk, dv in w:
                    e.wait_ge(self._sem(dk), dv)
                inst = fn(e)
                if last:
                    inst.then_inc(sem, 1)
            self.ops[self.ENG_STREAM[eng]].append(run)
        self.count[eng] += 1
        tok = (eng, self.count[eng])
        for b in reads:
            b.readers.append(tok)
        for b in writes:
            b.w = tok
            b.readers = []
        return tok

    def dma(self, queue, out_ap, in_ap, reads=(), writes=(), deps=(), **kw):
        return self.do(queue, lambda e: e.dma_start(out=out_ap, in_=in_ap, **kw),
                       reads=reads, writes=writes, deps=deps)

    def collective(self, fn):
        key = f"cc{sum(1 for k in self.sems if k.startswith('cc'))}"
        sem = self._sem(key)
        deps = self._flush_tokens("g") + self._pending.pop("g", [])
        waits = []
        for dk, dv in deps:
            if self.waited["g"].get(dk, -1) >= dv:
                continue
            waits.append((dk, dv))
            self.waited["g"][dk] = dv

        def run(e, waits=tuple(waits), fn=fn, sem=sem):
            for dk, dv in waits:
                e.wait_ge(self._sem(dk), dv)
            fn(e).then_inc(sem)

        self.ops["g"].append(run)
        tok = (key, 1)
        for k in ("t", "v", "s", "gc", "g", "q"):
            self._pending.setdefault(k, []).append(tok)
        return tok

    def barrier(self):
        toks = [(k, self.count[k]) for k in ("t", "v", "s", "gc")
                if self.count[k] > 0]
        toks += self._flush_tokens("g") + self._flush_tokens("q")
        for k in ("t", "v", "s", "gc", "g", "q"):
            self._pending.setdefault(k, []).extend(toks)

    def finalize(self, final_waits=()):
        nc = self.nc
        fw = [t for t in final_waits if t is not None and t[1] > 0]
        fw += self._flush_tokens("g") + self._flush_tokens("q")
        with nc.Block() as block:
            @block.tensor
            def _(e):
                for fn in self.ops["t"]:
                    fn(e)

            @block.vector
            def _(e):
                for fn in self.ops["v"]:
                    fn(e)

            @block.scalar
            def _(e):
                for fn in self.ops["s"]:
                    fn(e)

            @block.sync
            def _(e):
                for fn in self.ops["q"]:
                    fn(e)

            @block.gpsimd
            def _(e):
                for fn in self.ops["g"]:
                    fn(e)
                for dk, dv in fw:
                    e.wait_ge(self._sem(dk), dv)
        self.stack.close()
        return nc


def chunk_plan(T):
    out = []
    off = 0
    while off < T:
        c = min(512, T - off)
        out.append((off, c))
        off += c
    return out


def make_consts(p):
    c = {}
    nc = p.nc
    c["ident_bf"] = p.sbuf("ident_bf", [128, 128], BF16)
    c["ident_f"] = p.sbuf("ident_f", [128, 128], F32)
    i_bf = nc.dram_tensor("c_ident_bf", [128, 128], BF16, kind="ExternalInput")
    i_f = nc.dram_tensor("c_ident_f", [128, 128], F32, kind="ExternalInput")
    p.dma("q", c["ident_bf"][:], i_bf[:, :], writes=[c["ident_bf"]])
    p.dma("q", c["ident_f"][:], i_f[:, :], writes=[c["ident_f"]])
    o_bf = nc.dram_tensor("c_ones", [128, 128], BF16, kind="ExternalInput")
    c["ones_sq"] = p.sbuf("ones_sq", [128, 128], BF16)
    p.dma("q", c["ones_sq"][:], o_bf[:, :], writes=[c["ones_sq"]])
    cc = nc.dram_tensor("c_cols", [128, 2], F32, kind="ExternalInput")
    ccs = p.sbuf("c_cols_sb", [128, 2], F32)
    p.dma("q", ccs[:], cc[:, :], writes=[ccs])
    c["one"] = _SubBuf(ccs, 0)
    c["negone"] = _SubBuf(ccs, 1)
    return c


class _SubBuf:

    def __init__(self, buf, col):
        self._b = buf
        self._c = col

    def __getitem__(self, idx):
        ps, fs = idx
        return self._b.h[ps, self._c + fs.start:self._c + fs.stop]

    @property
    def w(self):
        return self._b.w

    @w.setter
    def w(self, v):
        self._b.w = v

    @property
    def readers(self):
        return self._b.readers

    @readers.setter
    def readers(self, v):
        self._b.readers = v


def host_consts():
    return {
        "c_ident_bf": np.eye(128, dtype=BF),
        "c_ident_f": np.eye(128, dtype=np.float32),
        "c_ones": np.ones((128, 128), dtype=BF),
        "c_cols": np.stack([np.ones(128, np.float32), -np.ones(128, np.float32)], axis=1),
    }


def emit_ln(p, cst, C, x_tiles, xn_tiles, nd=NDM, ps=None, scr=None):
    D = nd * 128
    ps_red, ps_rep = ps
    xb, sq, mean_r, rstd_r, rowb = (scr["xb"], scr["sq"], scr["mean"],
                                    scr["rstd"], scr["rowb"])
    for i in range(nd):
        p.do("s", lambda e, i=i: e.copy(xb[:, 0:C], x_tiles[i][0:128, 0:C]),
             reads=[x_tiles[i]], writes=[xb])
        p.do("t", lambda e, i=i: e.matmul(
            ps_red[0:1, 0:C], cst["ones_sq"][0:128, 0:1], xb[:, 0:C],
            start=(i == 0), stop=(i == nd - 1)),
            reads=[xb], writes=[ps_red])
    p.do("v", lambda e: e.tensor_scalar_mul(mean_r[0:1, 0:C], ps_red[0:1, 0:C], 1.0 / D),
         reads=[ps_red], writes=[mean_r])
    p.do("v", lambda e: e.tensor_copy(rowb[0:1, 0:C], mean_r[0:1, 0:C]),
         reads=[mean_r], writes=[rowb])
    p.do("t", lambda e: e.matmul(ps_rep[:, 0:C], cst["ones_sq"][0:1, 0:128],
                                 rowb[0:1, 0:C], start=True, stop=True),
         reads=[rowb], writes=[ps_rep])
    for i in range(nd):
        p.do("v", lambda e, i=i: e.tensor_sub(
            xn_tiles[i][0:128, 0:C], x_tiles[i][0:128, 0:C], ps_rep[:, 0:C]),
            reads=[x_tiles[i], ps_rep], writes=[xn_tiles[i]])
        p.do("s", lambda e, i=i: e.activation(sq[:, 0:C], xn_tiles[i][0:128, 0:C], AF.Square),
             reads=[xn_tiles[i]], writes=[sq])
        p.do("t", lambda e, i=i: e.matmul(
            ps_red[0:1, 0:C], cst["ones_sq"][0:128, 0:1], sq[:, 0:C],
            start=(i == 0), stop=(i == nd - 1)),
            reads=[sq], writes=[ps_red])
    p.do("v", lambda e: e.tensor_scalar(
        rstd_r[0:1, 0:C], ps_red[0:1, 0:C], 1.0 / D, 1e-5, ALU.mult, ALU.add),
        reads=[ps_red], writes=[rstd_r])
    p.do("s", lambda e: e.activation(rstd_r[0:1, 0:C], rstd_r[0:1, 0:C], AF.Sqrt),
         reads=[rstd_r], writes=[rstd_r])
    p.do("v", lambda e: e.reciprocal(rstd_r[0:1, 0:C], rstd_r[0:1, 0:C]),
         reads=[rstd_r], writes=[rstd_r])
    p.do("v", lambda e: e.tensor_copy(rowb[0:1, 0:C], rstd_r[0:1, 0:C]),
         reads=[rstd_r], writes=[rowb])
    p.do("t", lambda e: e.matmul(ps_rep[:, 0:C], cst["ones_sq"][0:1, 0:128],
                                 rowb[0:1, 0:C], start=True, stop=True),
         reads=[rowb], writes=[ps_rep])
    for i in range(nd):
        p.do("v", lambda e, i=i: e.tensor_mul(
            xn_tiles[i][0:128, 0:C], xn_tiles[i][0:128, 0:C], ps_rep[:, 0:C]),
            reads=[ps_rep], writes=[xn_tiles[i]])


def ln_scratch(p):
    return {
        "xb": p.sbuf("ln_xb", [128, 512], BF16),
        "sq": p.sbuf("ln_sq", [128, 512], BF16),
        "mean": p.sbuf("ln_mean", [1, 512], F32),
        "rstd": p.sbuf("ln_rstd", [1, 512], F32),
        "rowb": p.sbuf("ln_rowb", [1, 512], BF16),
    }


def load_weight_bf16(p, t, shape, name):
    k, m = shape
    assert k % 128 == 0
    tiles = []
    for i in range(k // 128):
        b = p.sbuf(f"w_{name}_{i}", [128, m], BF16)
        p.dma("g", b[:], t[i * 128:(i + 1) * 128, :], writes=[b])
        tiles.append(b)
    return tiles


def load_param_f32(p, t, shape2d, name):
    k, cols = shape2d
    assert k % 128 == 0
    tiles = []
    for i in range(k // 128):
        b = p.sbuf(f"p_{name}_{i}", [128, cols], F32)
        p.dma("q", b[:], t[i * 128:(i + 1) * 128, :], writes=[b])
        tiles.append(b)
    return tiles


def g_handles(nc):
    return dict(
        wr=nc.dram_tensor("g_wr", [DM, DM], F32, kind="ExternalInput"),
        wk=nc.dram_tensor("g_wk", [DM, DM], F32, kind="ExternalInput"),
        wv=nc.dram_tensor("g_wv", [DM, DM], F32, kind="ExternalInput"),
        mu=nc.dram_tensor("g_mu", [DM, 2], F32, kind="ExternalInput"))


def emit_g(p, cst, T_loc, x_in, x_out, gh):
    wr = load_weight_bf16(p, gh["wr"], (DM, DM), "g_wr")
    wk = load_weight_bf16(p, gh["wk"], (DM, DM), "g_wk")
    wv = load_weight_bf16(p, gh["wv"], (DM, DM), "g_wv")
    mu = load_param_f32(p, gh["mu"], (DM, 2), "g_mu")
    emit_g_sublayer(p, cst, T_loc, x_in, x_out, wr, wk, wv, mu)


def build_g_program(T_loc, with_store=True):
    p = Prog()
    nc = p.nc
    cst = make_consts(p)
    x_in = nc.dram_tensor("x_in", [DM, T_loc], F32, kind="ExternalInput")
    x_out = nc.dram_tensor("x_out", [DM, T_loc], F32, kind="ExternalOutput")
    emit_g(p, cst, T_loc, x_in, x_out, g_handles(nc))
    out_tok = p.count["g"]
    return p.finalize(final_waits=[("g", out_tok)])


def emit_g_sublayer(p, cst, T_loc, x_in, x_out, wr, wk, wv, mu):
    scr = ln_scratch(p)
    ps_red = p.psum("ps_red", [128, 512])
    ps_rep = p.psum("ps_rep", [128, 512])
    ps_a = p.psum("ps_a", [128, 512])
    ps_b = p.psum("ps_b", [128, 512])
    x_t = [p.sbuf(f"g_x{i}", [128, 512], F32) for i in range(NDM)]
    xn_t = [p.sbuf(f"g_xn{i}", [128, 513], BF16) for i in range(NDM)]
    xr_t = [p.sbuf(f"g_xr{i}", [128, 512], BF16) for i in range(NDM)]
    xk_t = [p.sbuf(f"g_xk{i}", [128, 512], BF16) for i in range(NDM)]
    tmp = p.sbuf("g_tmp", [128, 512], F32)
    r_t = [p.sbuf(f"g_r{i}", [128, 512], F32) for i in range(NDM)]
    ka_t = [p.sbuf(f"g_ka{i}", [128, 512], BF16) for i in range(NDM)]
    om_mu = p.sbuf("g_ommu", [128, 2 * NDM], F32)

    for i in range(NDM):
        p.do("v", lambda e, i=i: e.tensor_scalar(
            om_mu[:, 2 * i:2 * i + 2], mu[i][0:128, 0:2], -1.0, 1.0, ALU.mult, ALU.add),
            reads=[mu[i]], writes=[om_mu])
        p.do("v", lambda e, i=i: e.memset(xn_t[i][0:128, 0:1], 0.0),
             writes=[xn_t[i]])

    for ci, (off, C) in enumerate(chunk_plan(T_loc)):
        _g_chunk(p, cst, off, C, x_in, x_out, wr, wk, wv, mu, scr,
                 ps_red, ps_rep, ps_a, ps_b, x_t, xn_t, xr_t, xk_t, tmp,
                 r_t, ka_t, om_mu)


def _g_chunk(p, cst, off, C, x_in, x_out, wr, wk, wv, mu, scr,
             ps_red, ps_rep, ps_a, ps_b, x_t, xn_t, xr_t, xk_t, tmp,
             r_t, ka_t, om_mu):
        for i in range(NDM):
            p.dma("q", x_t[i][:, 0:C], x_in[i * 128:(i + 1) * 128, off:off + C],
                  writes=[x_t[i]])
        emit_ln_shifted(p, cst, C, x_t, xn_t, scr, (ps_red, ps_rep))
        for i in range(NDM):
            for (dst, mc) in ((xr_t, 0), (xk_t, 1)):
                p.do("v", lambda e, i=i, mc=mc: e.tensor_scalar_mul(
                    tmp[:, 0:C], xn_t[i][0:128, 0:C], om_mu[:, 2 * i + mc:2 * i + mc + 1]),
                    reads=[xn_t[i], om_mu], writes=[tmp])
                p.do("v", lambda e, i=i, dst=dst, mc=mc: e.scalar_tensor_tensor(
                    dst[i][0:128, 0:C], xn_t[i][0:128, 1:C + 1],
                    mu[i][0:128, mc:mc + 1], tmp[:, 0:C], ALU.mult, ALU.add),
                    reads=[xn_t[i], mu[i], tmp], writes=[dst[i]])
        for i in range(NDM):
            p.do("v", lambda e, i=i: e.tensor_copy(
                xn_t[i][0:128, 0:1], xn_t[i][0:128, C:C + 1]),
                reads=[xn_t[i]], writes=[xn_t[i]])
        for m in range(NDM):
            psm = ps_a if m % 2 == 0 else ps_b
            p.do_group("t", [
                (lambda e, m=m, k=k, psm=psm: e.matmul(
                    psm[:, 0:C], wr[k][0:128, m * 128:(m + 1) * 128],
                    xr_t[k][0:128, 0:C], start=(k == 0), stop=(k == NDM - 1)))
                for k in range(NDM)],
                reads=wr + xr_t, writes=[psm])
            p.do("s", lambda e, m=m, psm=psm: e.activation(
                r_t[m][0:128, 0:C], psm[:, 0:C], AF.Sigmoid),
                reads=[psm], writes=[r_t[m]])
        for m in range(NDM):
            psm = ps_a if m % 2 == 0 else ps_b
            p.do_group("t", [
                (lambda e, m=m, k=k, psm=psm: e.matmul(
                    psm[:, 0:C], wk[k][0:128, m * 128:(m + 1) * 128],
                    xk_t[k][0:128, 0:C], start=(k == 0), stop=(k == NDM - 1)))
                for k in range(NDM)],
                reads=wk + xk_t, writes=[psm])
            p.do("s", lambda e, m=m, psm=psm: e.activation(
                tmp[:, 0:C], psm[:, 0:C], AF.Relu),
                reads=[psm], writes=[tmp])
            p.do("s", lambda e, m=m: e.activation(
                ka_t[m][0:128, 0:C], tmp[:, 0:C], AF.Square),
                reads=[tmp], writes=[ka_t[m]])
        for m in range(NDM):
            psm = ps_a if m % 2 == 0 else ps_b
            p.do_group("t", [
                (lambda e, m=m, k=k, psm=psm: e.matmul(
                    psm[:, 0:C], wv[k][0:128, m * 128:(m + 1) * 128],
                    ka_t[k][0:128, 0:C], start=(k == 0), stop=(k == NDM - 1)))
                for k in range(NDM)],
                reads=wv + ka_t, writes=[psm])
            p.do("v", lambda e, m=m, psm=psm: e.scalar_tensor_tensor(
                tmp[:, 0:C], r_t[m][0:128, 0:C], 0.5, psm[:, 0:C],
                ALU.mult, ALU.mult),
                reads=[r_t[m], psm], writes=[tmp])
            p.do("v", lambda e, m=m: e.tensor_add(
                x_t[m][0:128, 0:C], x_t[m][0:128, 0:C], tmp[:, 0:C]),
                reads=[tmp], writes=[x_t[m]])
            p.dma("g", x_out[m * 128:(m + 1) * 128, off:off + C],
                  x_t[m][:, 0:C], reads=[x_t[m]])


def emit_ln_shifted(p, cst, C, x_tiles, xn_tiles, scr, ps):
    shifted = [_ColShift(b, 1) for b in xn_tiles]
    emit_ln(p, cst, C, x_tiles, shifted, nd=len(x_tiles), ps=ps, scr=scr)


class _ColShift:

    def __init__(self, buf, ofs):
        self._b = buf
        self._o = ofs

    def __getitem__(self, idx):
        ps, fs = idx
        return self._b.h[ps, fs.start + self._o: fs.stop + self._o]

    @property
    def w(self):
        return self._b.w

    @w.setter
    def w(self, v):
        self._b.w = v

    @property
    def readers(self):
        return self._b.readers

    @readers.setter
    def readers(self, v):
        self._b.readers = v


def shard_tokens(x_ld):
    L = x_ld.shape[0]
    T_own = L // NCORES
    out = []
    for k in range(NCORES):
        lo = k * T_own - HALO
        pad = max(0, -lo)
        sl = x_ld[max(0, lo):(k + 1) * T_own].T
        if pad:
            sl = np.concatenate([np.zeros((DM, pad), np.float32), sl], axis=1)
        out.append(np.ascontiguousarray(sl))
    return out


def unshard_tokens(parts):
    cols = [pp[:, HALO:] for pp in parts]
    return np.concatenate(cols, axis=1).T


class _RowOfs:

    def __init__(self, t, ofs):
        self._t = t
        self._o = ofs

    def __getitem__(self, idx):
        r, c = idx
        return self._t[r.start + self._o: r.stop + self._o, c]


class _SEView:

    def __init__(self, t, part):
        self._t = t
        self._p = part

    def __getitem__(self, idx):
        r, c = idx
        blk = r.start // DI
        x = r.start - blk * DI
        base = blk * 2 * DI + self._p * DI + x
        return self._t[base: base + (r.stop - r.start), c]


def build_mega(T_loc):
    p = Prog()
    nc = p.nc
    cst = make_consts(p)
    x_in = nc.dram_tensor("x_in", [DM, T_loc], F32, kind="ExternalInput")
    x_out = nc.dram_tensor("x_out", [DM, T_loc], F32, kind="ExternalOutput")
    x1 = nc.dram_tensor("x1_scr", [DM, T_loc], F32)
    x2 = nc.dram_tensor("x2_scr", [DM, T_loc], F32)
    z_hbm = nc.dram_tensor("z_scr", [DI, T_loc], BF16)
    u_hbm = nc.dram_tensor("u_scr", [DI, T_loc], BF16)
    cum_hbm = nc.dram_tensor("cum_scr", [DI, T_loc], BF16)
    yl_hbm = nc.dram_tensor("yl_scr", [DI, T_loc], BF16)
    pj_hbm = nc.dram_tensor("pj_scr", [128, T_loc], BF16)
    dt_hbm = nc.dram_tensor("dt_scr", [DI, T_loc], BF16)

    fh = f_handles(nc)
    gh = g_handles(nc)
    mh = {v: m_handles(nc, v, T_loc) for v in ("m0", "m1")}
    se = {v: (nc.dram_tensor(v + "_SE_loc", [2 * DI, DS], F32),
              nc.dram_tensor(v + "_SE_all", [NCORES * 2 * DI, DS], F32))
          for v in ("m0", "m1")}

    def emit_m(var, xa, xb):
        cfg = m_cfg(var, T_loc - HALO)
        se_loc, se_all = se[var]
        with p.scope():
            emit_ma(p, cst, T_loc, cfg, xa, _RowOfs(se_loc, 0),
                    _RowOfs(se_loc, DI), z_hbm, u_hbm, cum_hbm, yl_hbm,
                    pj_hbm, dt_hbm, mh[var])
        p.collective(lambda e: e.collective_compute(
            "AllGather", ALU.bypass,
            replica_groups=[list(range(NCORES))],
            ins=[se_loc[:, :]], outs=[se_all[:, :]]))
        p.barrier()
        with p.scope():
            emit_mb(p, cst, T_loc, cfg, xa, xb, _SEView(se_all, 0),
                    _SEView(se_all, 1), z_hbm, u_hbm, cum_hbm, yl_hbm,
                    pj_hbm, mh[var])

    with p.scope():
        emit_f_sublayer(p, cst, T_loc, x_in, x1, fh)
    emit_m("m0", x1, x2)
    with p.scope():
        emit_f_sublayer(p, cst, T_loc, x2, x1, fh)
    with p.scope():
        emit_g(p, cst, T_loc, x1, x2, gh)
    emit_m("m1", x2, x1)
    with p.scope():
        emit_g(p, cst, T_loc, x1, x_out, gh)
    return p.finalize(final_waits=[("g", p.count["g"]), ("q", p.count["q"])])


_PROG_CACHE = {}


def _prog(kind, T_loc):
    key = (kind, T_loc)
    if key not in _PROG_CACHE:
        if kind == "mega":
            _PROG_CACHE[key] = build_mega(T_loc)
        elif kind == "f":
            _PROG_CACHE[key] = build_f_program(T_loc)
        elif kind == "g":
            _PROG_CACHE[key] = build_g_program(T_loc)
        elif kind in ("m0a", "m1a"):
            _PROG_CACHE[key] = build_ma_program(T_loc, kind[:2])
        else:
            _PROG_CACHE[key] = build_mb_program(T_loc, kind[:2])
    return _PROG_CACHE[key]


def _ext_input_names(nc):
    names = set()
    for alloc in nc.m.functions[0].allocations:
        try:
            if alloc.kind == "ExternalInput":
                for m in alloc.memorylocations:
                    names.add(m.name)
        except AttributeError:
            pass
    return names


def _launch(nc, ins_list):
    names = _ext_input_names(nc)
    ins_list = [{k: np.ascontiguousarray(v) for k, v in d.items() if k in names}
                for d in ins_list]
    res = run_bass_kernel_spmd(nc, ins_list, core_ids=list(range(NCORES)))
    return res.results


def _mk_sel_f():
    sel = np.zeros((NET, NET * 128), BF)
    for e in range(NET):
        sel[e, e * 128:(e + 1) * 128] = 1
    return sel


def _mk_sel_m(cfg):
    NPR = cfg["npr"]
    sel = np.zeros((NPR, 33 * 128), BF)
    for i in range(DS):
        sel[cfg["rB"] + i, i * 128:(i + 1) * 128] = 1
        sel[cfg["rC"] + i, (DS + i) * 128:(DS + i + 1) * 128] = 1
    sel[NPR - 1, 2 * DS * 128:(2 * DS + 1) * 128] = 1
    return sel


def kernel(**inputs):
    ii = {k: np.asarray(v) for k, v in inputs.items()}
    x_full = ii["L0_attn_x"][0]
    L = x_full.shape[0]
    T_own = L // NCORES
    T_loc = T_own + HALO
    consts = host_consts()

    Wg, Wu, Wd = ii["L0_ffn_Wg"], ii["L0_ffn_Wu"], ii["L0_ffn_Wd"]
    Sg, Su, Sd = ii["L0_ffn_Sg"], ii["L0_ffn_Su"], ii["L0_ffn_Sd"]
    wg = np.zeros((NET, DM, HE), np.float32)
    wu = np.zeros((NET, DM, HE), np.float32)
    wd = np.zeros((NET, HE, DM), np.float32)
    for e in range(NE):
        wg[e] = Wg[e].T
        wu[e] = Wu[e].T
        wd[e] = Wd[e].T
    for sh in range(NS):
        SgT, SuT, SdT = Sg[sh].T, Su[sh].T, Sd[sh].T
        for q in range(4):
            e = NE + sh * 4 + q
            wg[e] = SgT[:, q * HE:(q + 1) * HE]
            wu[e] = SuT[:, q * HE:(q + 1) * HE]
            wd[e] = SdT[q * HE:(q + 1) * HE, :]
    f_w = dict(
        f_wr=np.ascontiguousarray(ii["L0_ffn_Wr"].T),
        f_rbias=np.broadcast_to(ii["L0_ffn_router_bias"], (128, NE)).copy(),
        f_sel=_mk_sel_f(), f_wg=wg, f_wu=wu, f_wd=wd)

    g_w = dict(
        g_wr=np.ascontiguousarray(ii["L1_ffn_W_r"].T),
        g_wk=np.ascontiguousarray(ii["L1_ffn_W_k"].T),
        g_wv=np.ascontiguousarray(ii["L1_ffn_W_v"].T),
        g_mu=np.stack([ii["L1_ffn_mu_r"], ii["L1_ffn_mu_k"]], axis=1))

    def m_weights(var, W_in, conv_w, conv_b, W_x, A_log, Dp, W_out, dtw, dtb):
        cfg = m_cfg(var, T_own)
        d = {
            var + "_win": np.ascontiguousarray(W_in.T),
            var + "_convw": np.ascontiguousarray(conv_w.reshape(DI, 4)),
            var + "_convb": conv_b.reshape(DI, 1),
            var + "_xproj": np.ascontiguousarray(W_x.T),
            var + "_alog": A_log,
            var + "_selbc": _mk_sel_m(cfg),
            var + "b_dparam": Dp.reshape(DI, 1),
            var + "_wout": np.ascontiguousarray(W_out.T),
            var + "_dtb": dtb.reshape(DI, 1),
        }
        if var == "m0":
            d[var + "_dtw"] = np.ascontiguousarray(dtw.T)
        else:
            d[var + "_wdt"] = dtw.reshape(DI, 1)
        return d

    m0_w = m_weights("m0", ii["L0_attn_in_proj_w"], ii["L0_attn_conv_w"],
                     ii["L0_attn_conv_b"], ii["L0_attn_x_proj_w"],
                     ii["L0_attn_A_log"], ii["L0_attn_D"],
                     ii["L0_attn_out_proj_w"], ii["L0_attn_dt_proj_w"],
                     ii["L0_attn_dt_proj_b"])
    m1_w = m_weights("m1", ii["L1_attn_W_in"], ii["L1_attn_conv_w"],
                     ii["L1_attn_conv_b"], ii["L1_attn_W_x"],
                     ii["L1_attn_A_log"], ii["L1_attn_D_param"],
                     ii["L1_attn_W_out"], ii["L1_attn_W_dt"],
                     ii["L1_attn_b_dt"])

    def m_core(var, c):
        mask = np.ones((128, T_loc), BF)
        oh = np.zeros((128, NCORES), np.float32)
        if c == 0:
            mask[:, 0:HALO] = 0.0
        else:
            oh[:, c - 1] = 1.0
        return {var + "_mask": mask, var + "b_oh": oh}

    xs = shard_tokens(x_full)

    outs = _launch(_prog("mega", T_loc),
                   [dict(x_in=xs[c], **f_w, **g_w, **m0_w, **m1_w, **consts,
                         **m_core("m0", c), **m_core("m1", c))
                    for c in range(NCORES)])
    xs = [outs[c]["x_out"] for c in range(NCORES)]

    out = unshard_tokens(xs)
    return out[None, :, :].astype(np.float32)


EGRP = 2
NET = 40


def f_handles(nc, pre="f"):
    return dict(
        wr=nc.dram_tensor(pre + "_wr", [DM, NE], F32, kind="ExternalInput"),
        rbias=nc.dram_tensor(pre + "_rbias", [128, NE], F32, kind="ExternalInput"),
        sel=nc.dram_tensor(pre + "_sel", [NET, NET * 128], BF16, kind="ExternalInput"),
        wg=nc.dram_tensor(pre + "_wg", [NET, DM, HE], F32, kind="ExternalInput"),
        wu=nc.dram_tensor(pre + "_wu", [NET, DM, HE], F32, kind="ExternalInput"),
        wd=nc.dram_tensor(pre + "_wd", [NET, HE, DM], F32, kind="ExternalInput"))


def build_f_program(T_loc):
    p = Prog()
    nc = p.nc
    cst = make_consts(p)
    x_in = nc.dram_tensor("x_in", [DM, T_loc], F32, kind="ExternalInput")
    x_out = nc.dram_tensor("x_out", [DM, T_loc], F32, kind="ExternalOutput")
    emit_f_sublayer(p, cst, T_loc, x_in, x_out, f_handles(nc))
    return p.finalize(final_waits=[("g", p.count["g"]), ("q", p.count["q"])])


def emit_f_sublayer(p, cst, T_loc, x_in, x_out, fh, pre="f"):
    nc = p.nc
    chunks = chunk_plan(T_loc)
    scr = ln_scratch(p)
    ps_red = p.psum(pre + "_psred", [128, 512])
    ps_rep = p.psum(pre + "_psrep", [128, 512])
    ps_g2 = [p.psum(pre + "_psg", [128, 512]) for _ in range(2)]
    ps_u2 = [p.psum(pre + "_psu", [128, 512]) for _ in range(2)]
    ps_cw2 = [p.psum(pre + "_pscw", [128, 512]) for _ in range(2)]
    ps_cw = ps_cw2[0]
    ps_o = [ps_red, ps_rep]

    w_r = load_weight_bf16(p, fh["wr"], (DM, NE), pre + "_wr")
    rbias = p.sbuf(pre + "_rbias_sb", [128, NE], F32)
    p.dma("q", rbias[:], fh["rbias"][:, :], writes=[rbias])
    sel = p.sbuf(pre + "_sel_sb", [NET, NET * 128], BF16)
    p.dma("q", sel[:], fh["sel"][:, :], writes=[sel])

    t_wg, t_wu, t_wd = fh["wg"], fh["wu"], fh["wd"]

    xn_all = [p.sbuf(pre + f"_xn{i}", [128, T_loc], BF16) for i in range(NDM)]
    cw_em = p.sbuf(pre + "_cwem", [NET, T_loc], BF16)
    out_acc = [p.sbuf(pre + f"_oa{i}", [128, T_loc], F32) for i in range(NDM)]
    x_t = [p.sbuf(pre + f"_x{i}", [128, 512], F32) for i in range(NDM)]

    wg_all = p.sbuf(pre + "_wgall", [128, NDM * EGRP * HE], BF16)
    wu_all = p.sbuf(pre + "_wuall", [128, NDM * EGRP * HE], BF16)
    wd_all = p.sbuf(pre + "_wdall", [128, 2 * EGRP * DM], BF16)
    h_sb = [p.sbuf(pre + f"_h{j}", [128, EGRP * T_loc], BF16) for j in range(2)]

    lg_sb = p.sbuf(pre + "_lg", [NE, 512], F32)
    lgT = x_t[0]
    s_sb = x_t[1]
    ind = x_t[2]
    ee = x_t[3]
    ww = x_t[4]
    tmp = x_t[5]
    tmp2 = [x_t[5], x_t[6]]
    mx = p.sbuf(pre + "_mx", [128, 8], F32)
    rs = p.sbuf(pre + "_rs", [128, 8], F32)
    cwT = p.sbuf(pre + "_cwT", [128, 512], F32)

    for i in range(NDM):
        p.do("v", lambda e, i=i: e.memset(out_acc[i][0:128, 0:T_loc], 0.0),
             writes=[out_acc[i]])
    p.do("v", lambda e: e.memset(cw_em[32:NET, 0:T_loc], 1.0), writes=[cw_em])
    p.do("v", lambda e: e.memset(ps_cw[:, 0:512], 0.0), writes=[ps_cw])

    for off, C in chunks:
        _f_router_chunk(p, cst, off, C, x_in, x_t, xn_all, cw_em, w_r, rbias,
                        scr, ps_red, ps_rep, ps_cw, lg_sb, lgT, s_sb, mx, ind,
                        ee, ww, rs, cwT)

    for g0 in range(0, NET, EGRP):
        for ei in range(EGRP):
            p.dma("g", wg_all[:, ei * NDM * HE:(ei + 1) * NDM * HE].rearrange(
                      "k (j h) -> k j h", j=NDM),
                  t_wg[g0 + ei].rearrange("(j k) h -> k j h", j=NDM),
                  writes=[wg_all])
            p.dma("g", wu_all[:, ei * NDM * HE:(ei + 1) * NDM * HE].rearrange(
                      "k (j h) -> k j h", j=NDM),
                  t_wu[g0 + ei].rearrange("(j k) h -> k j h", j=NDM),
                  writes=[wu_all])
            p.dma("g", wd_all[:, ei * 2 * DM:(ei + 1) * 2 * DM].rearrange(
                      "k (j m) -> k j m", j=2),
                  t_wd[g0 + ei].rearrange("(j k) m -> k j m", j=2),
                  writes=[wd_all])
        for ei in range(EGRP):
            par = (g0 + ei) % 2
            for off, C in chunks:
                _f_expert_chunk(p, cst, ei, g0 + ei, off, C, T_loc, xn_all,
                                cw_em, sel, wg_all, wu_all, h_sb,
                                ps_g2[par], ps_u2[par], ps_cw2[par], tmp2[par])
        for off, C in chunks:
            for m in range(NDM):
                psm = ps_o[m % 2]
                p.do_group("t", [
                    (lambda e, off=off, C=C, psm=psm, ei=ei, kj=kj,
                            co=(ei * 2 + kj) * DM + m * 128: e.matmul(
                        psm[:, 0:C], wd_all[0:128, co:co + 128],
                        h_sb[kj][0:128, ei * T_loc + off: ei * T_loc + off + C],
                        start=(ei == 0 and kj == 0), stop=(ei == EGRP - 1 and kj == 1)))
                    for ei in range(EGRP) for kj in range(2)],
                    reads=[wd_all, h_sb[0], h_sb[1]], writes=[psm])
                p.do("v", lambda e, m=m, off=off, C=C, psm=psm: e.tensor_add(
                    out_acc[m][0:128, off:off + C], out_acc[m][0:128, off:off + C],
                    psm[:, 0:C]),
                    reads=[psm], writes=[out_acc[m]])

    for off, C in chunks:
        for m in range(NDM):
            p.dma("q", x_t[m][:, 0:C], x_in[m * 128:(m + 1) * 128, off:off + C],
                  writes=[x_t[m]])
            p.do("v", lambda e, m=m, off=off, C=C: e.scalar_tensor_tensor(
                x_t[m][0:128, 0:C], out_acc[m][0:128, off:off + C], 0.5,
                x_t[m][0:128, 0:C], ALU.mult, ALU.add),
                reads=[out_acc[m]], writes=[x_t[m]])
            p.dma("g", x_out[m * 128:(m + 1) * 128, off:off + C], x_t[m][:, 0:C],
                  reads=[x_t[m]])


def _f_router_chunk(p, cst, off, C, x_in, x_t, xn_all, cw_em, w_r, rbias, scr,
                    ps_red, ps_rep, ps_cw, lg_sb, lgT, s_sb, mx, ind, ee, ww,
                    rs, cwT):
    nb = (C + 127) // 128
    for i in range(NDM):
        p.dma("q", x_t[i][:, 0:C], x_in[i * 128:(i + 1) * 128, off:off + C],
              writes=[x_t[i]])
    xn_view = [_ColShift(xn_all[i], off) for i in range(NDM)]
    emit_ln(p, cst, C, x_t, xn_view, nd=NDM, ps=(ps_red, ps_rep), scr=scr)
    p.do_group("t", [
        (lambda e, k=k, off=off, C=C: e.matmul(
            ps_cw[0:NE, 0:C], w_r[k][0:128, 0:NE],
            xn_all[k][0:128, off:off + C], start=(k == 0), stop=(k == NDM - 1)))
        for k in range(NDM)],
        reads=w_r + xn_all, writes=[ps_cw])
    p.do("s", lambda e, C=C: e.copy(lg_sb[0:NE, 0:C], ps_cw[0:NE, 0:C]),
         reads=[ps_cw], writes=[lg_sb])
    for b in range(nb):
        bc = min(128, C - b * 128)
        p.do("t", lambda e, b=b, bc=bc: e.transpose(
            ps_cw[0:bc, b * NE:(b + 1) * NE], lg_sb[0:NE, b * 128:b * 128 + bc],
            cst["ident_f"][0:NE, 0:NE]),
            reads=[lg_sb, cst["ident_f"]], writes=[ps_cw])
    p.do("v", lambda e, nb=nb: e.tensor_copy(lgT[0:128, 0:nb * NE], ps_cw[:, 0:nb * NE]),
         reads=[ps_cw], writes=[lgT])
    p.do("v", lambda e, nb=nb: e.tensor_tensor(
        s_sb[0:128, 0:nb * NE], lgT[0:128, 0:nb * NE],
        rbias[:].unsqueeze(1).broadcast_to([128, nb, NE]),
        ALU.add), reads=[lgT, rbias], writes=[s_sb])
    for r in range(4):
        p.do("v", lambda e, nb=nb: e.tensor_reduce(
            mx[:, 0:nb],
            s_sb[0:128, 0:nb * NE].rearrange("p (b e) -> p b e", e=NE),
            AX.X, ALU.max),
            reads=[s_sb], writes=[mx])
        p.do("v", lambda e, nb=nb: e.tensor_tensor(
            ee[0:128, 0:nb * NE], s_sb[0:128, 0:nb * NE],
            mx[:, 0:nb].unsqueeze(2).broadcast_to([128, nb, NE]),
            ALU.is_ge), reads=[s_sb, mx], writes=[ee])
        p.do("v", lambda e, nb=nb: e.scalar_tensor_tensor(
            s_sb[0:128, 0:nb * NE], ee[0:128, 0:nb * NE], -1e30,
            s_sb[0:128, 0:nb * NE], ALU.mult, ALU.add),
            reads=[ee], writes=[s_sb])
    p.do("v", lambda e, nb=nb: e.tensor_scalar(
        ind[0:128, 0:nb * NE], s_sb[0:128, 0:nb * NE], -1e29, None, ALU.is_lt),
        reads=[s_sb], writes=[ind])
    p.do("v", lambda e, nb=nb: e.tensor_reduce(
        mx[:, 0:nb], lgT[0:128, 0:nb * NE].rearrange("p (b e) -> p b e", e=NE),
        AX.X, ALU.max), reads=[lgT], writes=[mx])
    p.do("v", lambda e, nb=nb: e.tensor_tensor(
        ee[0:128, 0:nb * NE], lgT[0:128, 0:nb * NE],
        mx[:, 0:nb].unsqueeze(2).broadcast_to([128, nb, NE]),
        ALU.subtract), reads=[lgT, mx], writes=[ee])
    p.do("s", lambda e, nb=nb: e.activation(ee[0:128, 0:nb * NE], ee[0:128, 0:nb * NE], AF.Exp),
         reads=[ee], writes=[ee])
    p.do("v", lambda e, nb=nb: e.tensor_mul(ww[0:128, 0:nb * NE], ee[0:128, 0:nb * NE],
                                            ind[0:128, 0:nb * NE]),
         reads=[ee, ind], writes=[ww])
    p.do("v", lambda e, nb=nb: e.tensor_reduce(
        rs[:, 0:nb], ww[0:128, 0:nb * NE].rearrange("p (b e) -> p b e", e=NE),
        AX.X, ALU.add), reads=[ww], writes=[rs])
    p.do("v", lambda e, nb=nb: e.reciprocal(rs[:, 4:4 + nb], rs[:, 0:nb]),
         reads=[rs], writes=[rs])
    p.do("v", lambda e, nb=nb: e.tensor_tensor(
        cwT[0:128, 0:nb * NE], ww[0:128, 0:nb * NE],
        rs[:, 4:4 + nb].unsqueeze(2).broadcast_to([128, nb, NE]),
        ALU.mult), reads=[ww, rs], writes=[cwT])
    for b in range(nb):
        bc = min(128, C - b * 128)
        p.do("t", lambda e, b=b, bc=bc: e.transpose(
            ps_cw[0:NE, b * 128:b * 128 + bc], cwT[0:bc, b * NE:(b + 1) * NE],
            cst["ident_f"][0:bc, 0:bc]),
            reads=[cwT, cst["ident_f"]], writes=[ps_cw])
    p.do("s", lambda e, off=off, C=C: e.copy(cw_em[0:NE, off:off + C], ps_cw[0:NE, 0:C]),
         reads=[ps_cw], writes=[cw_em])


def _f_expert_chunk(p, cst, ei, e_abs, off, C, T_loc, xn_all, cw_em, sel,
                    wg_all, wu_all, h_sb, ps_g, ps_u, ps_cw, tmp):
    p.do("t", lambda e: e.matmul(
        ps_cw[:, 0:C], sel[0:NET, e_abs * 128:(e_abs + 1) * 128],
        cw_em[0:NET, off:off + C], start=True, stop=True),
        reads=[sel, cw_em], writes=[ps_cw])
    for hj in range(2):
        p.do_group("t", [
            (lambda e, k=k, co=(ei * NDM + k) * HE + hj * 128: e.matmul(
                ps_g[:, 0:C], wg_all[0:128, co:co + 128],
                xn_all[k][0:128, off:off + C], start=(k == 0), stop=(k == NDM - 1)))
            for k in range(NDM)],
            reads=[wg_all] + xn_all, writes=[ps_g])
        p.do_group("t", [
            (lambda e, k=k, co=(ei * NDM + k) * HE + hj * 128: e.matmul(
                ps_u[:, 0:C], wu_all[0:128, co:co + 128],
                xn_all[k][0:128, off:off + C], start=(k == 0), stop=(k == NDM - 1)))
            for k in range(NDM)],
            reads=[wu_all] + xn_all, writes=[ps_u])
        p.do("s", lambda e: e.activation(tmp[0:128, 0:C], ps_g[:, 0:C], AF.Silu),
             reads=[ps_g], writes=[tmp])
        p.do("v", lambda e: e.tensor_mul(tmp[0:128, 0:C], tmp[0:128, 0:C], ps_u[:, 0:C]),
             reads=[ps_u], writes=[tmp])
        p.do("v", lambda e, hj=hj: e.tensor_mul(
            h_sb[hj][0:128, ei * T_loc + off: ei * T_loc + off + C],
            tmp[0:128, 0:C], ps_cw[:, 0:C]),
            reads=[tmp, ps_cw], writes=[h_sb[hj]])


def m_cfg(variant, T_own):
    if variant == "m0":
        return dict(var="m0", npr=96, rB=64, rC=80, pstart=3, spos=T_own + 2)
    return dict(var="m1", npr=33, rB=0, rC=16, pstart=7, spos=T_own + 6)


def m_handles(nc, var, T_loc):
    npr = m_cfg(var, 0)["npr"]
    d = dict(
        win=nc.dram_tensor(var + "_win", [DM, 2 * DI], F32, kind="ExternalInput"),
        xproj=nc.dram_tensor(var + "_xproj", [DI, npr], F32, kind="ExternalInput"),
        convw=nc.dram_tensor(var + "_convw", [DI, 4], F32, kind="ExternalInput"),
        convb=nc.dram_tensor(var + "_convb", [DI, 1], F32, kind="ExternalInput"),
        dtb=nc.dram_tensor(var + "_dtb", [DI, 1], F32, kind="ExternalInput"),
        alog=nc.dram_tensor(var + "_alog", [DI, DS], F32, kind="ExternalInput"),
        selbc=nc.dram_tensor(var + "_selbc", [npr, (2 * DS + 1) * 128], BF16,
                             kind="ExternalInput"),
        mask=nc.dram_tensor(var + "_mask", [128, T_loc], BF16,
                            kind="ExternalInput"),
        wout=nc.dram_tensor(var + "_wout", [DI, DM], F32, kind="ExternalInput"),
        dparam=nc.dram_tensor(var + "b_dparam", [DI, 1], F32,
                              kind="ExternalInput"),
        oh=nc.dram_tensor(var + "b_oh", [128, NCORES], F32,
                          kind="ExternalInput"))
    if var == "m0":
        d["dtw"] = nc.dram_tensor(var + "_dtw", [DTR, DI], F32,
                                  kind="ExternalInput")
    else:
        d["dtw"] = nc.dram_tensor(var + "_wdt", [DI, 1], F32,
                                  kind="ExternalInput")
    return d


def build_ma_program(T_loc, variant):
    p = Prog()
    nc = p.nc
    cfg = m_cfg(variant, T_loc - HALO)
    cst = make_consts(p)
    x_in = nc.dram_tensor("x_in", [DM, T_loc], F32, kind="ExternalInput")
    S_out = nc.dram_tensor("S_out", [DI, DS], F32, kind="ExternalOutput")
    E_out = nc.dram_tensor("E_out", [DI, DS], F32, kind="ExternalOutput")
    z_hbm = nc.dram_tensor("z_scr", [DI, T_loc], BF16, kind="ExternalOutput")
    u_hbm = nc.dram_tensor("u_scr", [DI, T_loc], BF16, kind="ExternalOutput")
    cum_hbm = nc.dram_tensor("cum_scr", [DI, T_loc], BF16, kind="ExternalOutput")
    yl_hbm = nc.dram_tensor("yl_scr", [DI, T_loc], BF16, kind="ExternalOutput")
    pj_hbm = nc.dram_tensor("pj_scr", [128, T_loc], BF16, kind="ExternalOutput")
    dt_hbm = nc.dram_tensor("dt_scr", [DI, T_loc], BF16, kind="ExternalOutput")
    emit_ma(p, cst, T_loc, cfg, x_in, S_out, E_out, z_hbm, u_hbm, cum_hbm,
            yl_hbm, pj_hbm, dt_hbm, m_handles(nc, variant, T_loc))
    return p.finalize(final_waits=[("g", p.count["g"]), ("q", p.count["q"])])


def emit_ma(p, cst, T_loc, cfg, x_in, S_out, E_out, z_hbm, u_hbm, cum_hbm,
            yl_hbm, pj_hbm, dt_hbm, mh):
    nc = p.nc
    var = cfg["var"]
    NPR = cfg["npr"]
    pstart, spos = cfg["pstart"], cfg["spos"]
    WF = T_loc - pstart
    chunks = chunk_plan(T_loc)
    scr = ln_scratch(p)
    ps_big = p.psum(var + "_psbig", [128, 2560])
    ps_s = [p.psum(var + f"_pss{i}", [128, 512]) for i in range(3)]
    bank = [Buf(ps_big.h) for _ in range(2)]
    class _BankView(Buf):
        def __init__(self, h, ofs):
            super().__init__(h)
            self._o = ofs
        def __getitem__(self, idx):
            ps, fs = idx
            return self.h[ps, self._o + fs.start:self._o + fs.stop]
    bank = [_BankView(ps_big.h, 0), _BankView(ps_big.h, 512)]

    t_win = mh["win"]
    win2 = [p.sbuf(var + f"_win2{i}", [128, NDM * 512], BF16) for i in range(2)]
    xp_sb = [p.sbuf(var + f"_xp{j}", [128, NPR], BF16) for j in range(NDI)]
    t_xp = mh["xproj"]
    for j in range(NDI):
        p.dma("g", xp_sb[j][:], t_xp[j * 128:(j + 1) * 128, :], writes=[xp_sb[j]])
    convw = load_param_f32(p, mh["convw"], (DI, 4), var + "_convw")
    convb = load_param_f32(p, mh["convb"], (DI, 1), var + "_convb")
    dtb = load_param_f32(p, mh["dtb"], (DI, 1), var + "_dtb")
    alog = load_param_f32(p, mh["alog"], (DI, DS), var + "_alog")
    if var == "m0":
        dtw = p.sbuf(var + "_dtw_sb", [DTR, DI], BF16)
        p.dma("g", dtw[:], mh["dtw"][:, :], writes=[dtw])
    else:
        dtw = load_param_f32(p, mh["dtw"], (DI, 1), var + "_wdt")
    selbc = p.sbuf(var + "_selbc_sb", [NPR, (2 * DS + 1) * 128], BF16)
    p.dma("q", selbc[:], mh["selbc"][:, :], writes=[selbc])
    mask_sb = p.sbuf(var + "_mask_sb", [128, T_loc], BF16)
    p.dma("q", mask_sb[:], mh["mask"][:, :], writes=[mask_sb])

    A_sb = [p.sbuf(var + f"_A{j}", [128, DS], F32) for j in range(NDI)]
    for j in range(NDI):
        p.do("s", lambda e, j=j: e.activation(A_sb[j][0:128, 0:DS], alog[j][0:128, 0:DS], AF.Exp),
             reads=[alog[j]], writes=[A_sb[j]])
        p.do("v", lambda e, j=j: e.tensor_scalar_mul(A_sb[j][0:128, 0:DS], A_sb[j][0:128, 0:DS], -1.0),
             reads=[], writes=[A_sb[j]])

    x_t = [p.sbuf(var + f"_x{i}", [128, 512], F32) for i in range(NDM)]
    xn_t = [p.sbuf(var + f"_xn{i}", [128, 512], BF16) for i in range(NDM)]
    xs_t = [p.sbuf(var + f"_xs{j}", [128, 515], BF16) for j in range(NDI)]
    ub_t = [p.sbuf(var + f"_ub{j}", [128, 512], BF16) for j in range(NDI)]
    cum_col = [p.sbuf(var + f"_cc{j}", [128, 1], F32) for j in range(NDI)]
    projb = p.sbuf(var + "_projb", [NPR, 512], BF16)
    zb = [p.sbuf(var + f"_zb{i}", [128, 512], BF16) for i in range(2)]
    cacc = [p.sbuf(var + f"_cacc{i}", [128, 512], F32) for i in range(2)]
    dtraw1 = p.sbuf(var + "_dtraw", [128, 512], F32)
    dtraw = [dtraw1, dtraw1]
    sp11 = p.sbuf(var + "_sp1", [128, 512], F32)
    sp1 = [sp11, sp11]
    dt_f = [p.sbuf(var + f"_dtf{i}", [128, 512], F32) for i in range(2)]
    dt_b = [p.sbuf(var + f"_dtb2{i}", [128, 512], BF16) for i in range(2)]
    cumf1 = p.sbuf(var + "_cumf", [128, 512], F32)
    cumf = [cumf1, cumf1]
    cumb = [p.sbuf(var + f"_cumb{i}", [128, 512], BF16) for i in range(2)]
    E_sb = [p.sbuf(var + f"_E{j}", [128, DS], F32) for j in range(NDI)]

    cst["zeros512"] = p.sbuf(var + "_zeros", [128, 512], F32)
    p.do("v", lambda e: e.memset(cst["zeros512"][0:128, 0:512], 0.0),
         writes=[cst["zeros512"]])
    for j in range(NDI):
        p.do("v", lambda e, j=j: e.memset(cum_col[j][0:128, 0:1], 0.0), writes=[cum_col[j]])
        p.do("v", lambda e, j=j: e.memset(xs_t[j][0:128, 0:3], 0.0), writes=[xs_t[j]])

    for off, C in chunks:
        _ma1_chunk(p, cst, cfg, off, C, x_in, z_hbm, u_hbm, cum_hbm, pj_hbm,
                   dt_hbm, (t_win, win2), xp_sb, convw, convb, dtb, dtw, selbc,
                   mask_sb, A_sb, x_t, xn_t, xs_t, ub_t, cum_col, projb, zb,
                   cacc, dtraw, sp1, dt_f, dt_b, cumf, cumb, E_sb, scr,
                   ps_s, bank)
    p.barrier()

    JG = 4
    H0 = 512
    halves = []
    _o = 0
    while _o < WF:
        halves.append((_o, min(H0, WF - _o)))
        _o += min(H0, WF - _o)
    dtg = [p.sbuf(var + f"_dtg{i}", [128, WF], BF16) for i in range(JG)]
    ug2 = [p.sbuf(var + f"_ug{i}", [128, WF], BF16) for i in range(2)]
    ug = [ug2[0], ug2[1], ug2[0], ug2[1]]
    wg = [p.sbuf(var + f"_wg2{i}", [128, WF], BF16) for i in range(JG)]
    brep = [p.sbuf(var + f"_brep2{i}", [128, H0], BF16) for i in range(2)]
    crep = [p.sbuf(var + f"_crep2{i}", [128, H0], BF16) for i in range(2)]
    pjf = p.sbuf(var + "_pjf", [NPR, WF], BF16)
    dA4 = [p.sbuf(var + f"_dA2{i}", [128, H0], F32) for i in range(4)]
    dA = dA4
    in1 = [p.sbuf(var + f"_in12{i}", [128, H0], BF16) for i in range(4)]
    h_t = [p.sbuf(var + f"_h2{i}", [128, H0], BF16) for i in range(4)]
    yp = [p.sbuf(var + f"_yp2{i}", [128, H0], BF16) for i in range(4)]
    ylb = [p.sbuf(var + f"_ylb2{i}", [128, H0], BF16) for i in range(JG)]
    S_sb = [p.sbuf(var + f"_S{j}", [128, DS], F32) for j in range(NDI)]
    state = [p.sbuf(var + f"_st{j}", [128, DS], F32) for j in range(NDI)]

    class _YView(Buf):
        def __init__(self, h, ofs):
            super().__init__(h)
            self._o = ofs
        def __getitem__(self, idx):
            ps, fs = idx
            return self.h[ps, self._o + fs.start:self._o + fs.stop]
    ps_yv = [_YView(ps_big.h, 512 * i) for i in range(4)]
    ps_bc2 = _YView(ps_big.h, 2048)

    p.dma("q", pjf[0:NPR, 0:WF], pj_hbm[0:NPR, pstart:T_loc], writes=[pjf])
    for j0 in range(0, NDI, JG):
        for i in range(JG):
            j = j0 + i
            p.dma("q", dtg[i][:, 0:WF], dt_hbm[j * 128:(j + 1) * 128, pstart:T_loc],
                  writes=[dtg[i]])
            p.dma("q", ug[i][:, 0:WF], u_hbm[j * 128:(j + 1) * 128, pstart:T_loc],
                  writes=[ug[i]])
            p.do("v", lambda e, i=i: e.tensor_mul(wg[i][0:128, 0:WF],
                                                  dtg[i][0:128, 0:WF],
                                                  ug[i][0:128, 0:WF]),
                 reads=[dtg[i], ug[i]], writes=[wg[i]])
        for (hof, hw) in halves:
            for n in range(DS):
                bp = n % 2
                psB = ps_s[bp]
                psC = ps_s[2] if bp == 0 else ps_bc2
                p.do("t", lambda e, n=n, hof=hof, hw=hw, psB=psB: e.matmul(
                    psB[0:128, 0:hw], selbc[0:NPR, n * 128:(n + 1) * 128],
                    pjf[0:NPR, hof:hof + hw], start=True, stop=True),
                    reads=[selbc, pjf], writes=[psB])
                p.do("s", lambda e, hw=hw, bp=bp, psB=psB: e.copy(
                    brep[bp][0:128, 0:hw], psB[0:128, 0:hw]),
                    reads=[psB], writes=[brep[bp]])
                p.do("t", lambda e, n=n, hof=hof, hw=hw, psC=psC: e.matmul(
                    psC[0:128, 0:hw], selbc[0:NPR, (DS + n) * 128:(DS + n + 1) * 128],
                    pjf[0:NPR, hof:hof + hw], start=True, stop=True),
                    reads=[selbc, pjf], writes=[psC])
                p.do("s", lambda e, hw=hw, bp=bp, psC=psC: e.copy(
                    crep[bp][0:128, 0:hw], psC[0:128, 0:hw]),
                    reads=[psC], writes=[crep[bp]])
                for i in range(JG):
                    j = j0 + i
                    pa = i
                    p.do("s", lambda e, i=i, j=j, n=n, pa=pa, hof=hof, hw=hw: e.activation(
                        dA[pa][0:128, 0:hw], dtg[i][0:128, hof:hof + hw], AF.Exp,
                        bias=0.0, scale=A_sb[j][0:128, n:n + 1]),
                        reads=[dtg[i], A_sb[j]], writes=[dA[pa]])
                    p.do("v", lambda e, i=i, pa=pa, bp=bp, hof=hof, hw=hw: e.tensor_mul(
                        in1[pa][0:128, 0:hw], wg[i][0:128, hof:hof + hw],
                        brep[bp][0:128, 0:hw]),
                        reads=[wg[i], brep[bp]], writes=[in1[pa]])
                    if hof == 0:
                        p.do("v", lambda e, pa=pa, hw=hw: e.tensor_tensor_scan(
                            h_t[pa][0:128, 0:hw], dA[pa][0:128, 0:hw],
                            in1[pa][0:128, 0:hw], 0.0, ALU.mult, ALU.add),
                            reads=[dA[pa], in1[pa]], writes=[h_t[pa]])
                    else:
                        p.do("v", lambda e, j=j, n=n, pa=pa, hw=hw: e.tensor_tensor_scan(
                            h_t[pa][0:128, 0:hw], dA[pa][0:128, 0:hw],
                            in1[pa][0:128, 0:hw], state[j][0:128, n:n + 1],
                            ALU.mult, ALU.add),
                            reads=[dA[pa], in1[pa], state[j]], writes=[h_t[pa]])
                    if hof + hw < WF:
                        p.do("v", lambda e, j=j, n=n, pa=pa, hw=hw: e.tensor_copy(
                            state[j][0:128, n:n + 1], h_t[pa][0:128, hw - 1:hw]),
                            reads=[h_t[pa]], writes=[state[j]])
                    if hof <= spos - pstart < hof + hw:
                        sc = spos - pstart - hof
                        p.do("v", lambda e, j=j, n=n, pa=pa, sc=sc: e.tensor_copy(
                            S_sb[j][0:128, n:n + 1], h_t[pa][0:128, sc:sc + 1]),
                            reads=[h_t[pa]], writes=[S_sb[j]])
                    p.do("v", lambda e, pa=pa, bp=bp, hw=hw: e.tensor_mul(
                        yp[pa][0:128, 0:hw], h_t[pa][0:128, 0:hw], crep[bp][0:128, 0:hw]),
                        reads=[h_t[pa], crep[bp]], writes=[yp[pa]])
                    p.do("t", lambda e, n=n, i=i, pa=pa, hw=hw: e.matmul(
                        ps_yv[i].h[0:128, ps_yv[i]._o:ps_yv[i]._o + hw],
                        cst["ident_bf"][0:128, 0:128], yp[pa][0:128, 0:hw],
                        start=(n == 0), stop=(n == DS - 1)),
                        reads=[cst["ident_bf"], yp[pa]], writes=[ps_yv[i]])
            for i in range(JG):
                j = j0 + i
                p.do("s", lambda e, i=i, hw=hw: e.copy(
                    ylb[i][0:128, 0:hw],
                    ps_yv[i].h[0:128, ps_yv[i]._o:ps_yv[i]._o + hw]),
                    reads=[ps_yv[i]], writes=[ylb[i]])
                p.dma("g", yl_hbm[j * 128:(j + 1) * 128, pstart + hof:pstart + hof + hw],
                      ylb[i][:, 0:hw], reads=[ylb[i]],
                      **({"allow_slow_non_contiguous": True} if hw < 8 else {}))
    for j in range(NDI):
        p.dma("g", S_out[j * 128:(j + 1) * 128, :], S_sb[j][:, 0:DS], reads=[S_sb[j]])
        p.dma("g", E_out[j * 128:(j + 1) * 128, :], E_sb[j][:, 0:DS], reads=[E_sb[j]])


def _ma1_chunk(p, cst, cfg, off, C, x_in, z_hbm, u_hbm, cum_hbm, pj_hbm,
               dt_hbm, win_pack, xp_sb, convw, convb, dtb, dtw, selbc, mask_sb,
               A_sb, x_t, xn_t, xs_t, ub_t, cum_col, projb, zb, cacc, dtraw,
               sp1, dt_f, dt_b, cumf, cumb, E_sb, scr, ps_s, bank):
    var, NPR = cfg["var"], cfg["npr"]
    pstart, spos = cfg["pstart"], cfg["spos"]
    c0 = max(off, pstart)
    W = off + C - c0
    s0 = c0 - off
    for i in range(NDM):
        p.dma("q", x_t[i][:, 0:C], x_in[i * 128:(i + 1) * 128, off:off + C],
              writes=[x_t[i]])
    emit_ln(p, cst, C, x_t, xn_t, nd=NDM, ps=(ps_s[0], ps_s[1]), scr=scr)
    t_win, win2 = win_pack
    for mq in range(8):
        wb = win2[mq % 2]
        p.dma("g", wb[:, :].rearrange("k (j m) -> k j m", j=NDM),
              t_win[:, mq * 512:(mq + 1) * 512].rearrange("(j k) m -> k j m", j=NDM),
              writes=[wb])
        for mi in range(4):
            m = mq * 4 + mi
            psm = bank[m % 2]
            p.do_group("t", [
                (lambda e, m=m, mi=mi, k=k, psm=psm, C=C, wb=wb: e.matmul(
                    psm[0:128, 0:C], wb[0:128, k * 512 + mi * 128:k * 512 + (mi + 1) * 128],
                    xn_t[k][0:128, 0:C], start=(k == 0), stop=(k == NDM - 1)))
                for k in range(NDM)],
                reads=[wb] + xn_t, writes=[psm])
            if m < NDI:
                p.do("s", lambda e, m=m, psm=psm, C=C: e.copy(
                    xs_t[m][0:128, 3:3 + C], psm[0:128, 0:C]),
                    reads=[psm], writes=[xs_t[m]])
            else:
                zpar = zb[m % 2]
                p.do("s", lambda e, psm=psm, C=C, zpar=zpar: e.copy(
                    zpar[0:128, 0:C], psm[0:128, 0:C]),
                    reads=[psm], writes=[zpar])
                p.dma("g", z_hbm[(m - NDI) * 128:(m - NDI + 1) * 128, off:off + C],
                      zpar[:, 0:C], reads=[zpar])
    for j in range(NDI):
        pa = j % 2
        p.do("v", lambda e, j=j, C=C, pa=pa: e.tensor_scalar_mul(
            cacc[pa][0:128, 0:C], xs_t[j][0:128, 0:C], convw[j][0:128, 0:1]),
            reads=[xs_t[j], convw[j]], writes=[cacc[pa]])
        for tap in range(1, 4):
            p.do("v", lambda e, j=j, tap=tap, C=C, pa=pa: e.scalar_tensor_tensor(
                cacc[pa][0:128, 0:C], xs_t[j][0:128, tap:tap + C],
                convw[j][0:128, tap:tap + 1], cacc[pa][0:128, 0:C],
                ALU.mult, ALU.add),
                reads=[xs_t[j], convw[j]], writes=[cacc[pa]])
        p.do("s", lambda e, j=j, C=C, pa=pa: e.activation(
            ub_t[j][0:128, 0:C], cacc[pa][0:128, 0:C], AF.Silu,
            bias=convb[j][0:128, 0:1], scale=1.0),
            reads=[cacc[pa], convb[j]], writes=[ub_t[j]])
        p.dma("g", u_hbm[j * 128:(j + 1) * 128, off:off + C], ub_t[j][:, 0:C],
              reads=[ub_t[j]])
        p.do("v", lambda e, j=j, C=C: e.tensor_copy(
            xs_t[j][0:128, 0:3], xs_t[j][0:128, C:C + 3]),
            reads=[], writes=[xs_t[j]])
    p.do_group("t", [
        (lambda e, k=k, C=C: e.matmul(
            ps_s[2][0:NPR, 0:C], xp_sb[k][0:128, 0:NPR], ub_t[k][0:128, 0:C],
            start=(k == 0), stop=(k == NDI - 1)))
        for k in range(NDI)],
        reads=xp_sb + ub_t, writes=[ps_s[2]])
    p.do("s", lambda e, C=C: e.copy(projb[0:NPR, 0:C], ps_s[2][0:NPR, 0:C]),
         reads=[ps_s[2]], writes=[projb])
    p.dma("g", pj_hbm[0:NPR, off:off + C], projb[0:NPR, 0:C], reads=[projb])
    for j in range(NDI):
        pa = j % 2
        psd = bank[pa]
        if var == "m0":
            p.do("t", lambda e, j=j, C=C, psd=psd: e.matmul(
                psd[0:128, 0:C], dtw[0:DTR, j * 128:(j + 1) * 128],
                projb[0:DTR, 0:C], start=True, stop=True),
                reads=[dtw, projb], writes=[psd])
            p.do("v", lambda e, C=C, pa=pa, psd=psd: e.tensor_copy(
                dtraw[pa][0:128, 0:C], psd[0:128, 0:C]),
                reads=[psd], writes=[dtraw[pa]])
        else:
            p.do("t", lambda e, j=j, C=C, psd=psd: e.matmul(
                psd[0:128, 0:C], selbc[0:NPR, 2 * DS * 128:(2 * DS + 1) * 128],
                projb[0:NPR, 0:C], start=True, stop=True),
                reads=[selbc, projb], writes=[psd])
            p.do("v", lambda e, j=j, C=C, pa=pa, psd=psd: e.tensor_scalar_mul(
                dtraw[pa][0:128, 0:C], psd[0:128, 0:C], dtw[j][0:128, 0:1]),
                reads=[psd, dtw[j]], writes=[dtraw[pa]])
        p.do("s", lambda e, j=j, C=C, pa=pa: e.activation(
            sp1[pa][0:128, 0:C], dtraw[pa][0:128, 0:C], AF.Abs,
            bias=dtb[j][0:128, 0:1], scale=1.0),
            reads=[dtraw[pa], dtb[j]], writes=[sp1[pa]])
        p.do("s", lambda e, C=C, pa=pa: e.activation(
            sp1[pa][0:128, 0:C], sp1[pa][0:128, 0:C], AF.Exp,
            bias=0.0, scale=cst["negone"][0:128, 0:1]),
            reads=[sp1[pa], cst["negone"]], writes=[sp1[pa]])
        p.do("s", lambda e, C=C, pa=pa: e.activation(
            sp1[pa][0:128, 0:C], sp1[pa][0:128, 0:C], AF.Ln,
            bias=cst["one"][0:128, 0:1], scale=1.0),
            reads=[sp1[pa], cst["one"]], writes=[sp1[pa]])
        p.do("s", lambda e, j=j, C=C, pa=pa: e.activation(
            dt_f[pa][0:128, 0:C], dtraw[pa][0:128, 0:C], AF.Relu,
            bias=dtb[j][0:128, 0:1], scale=1.0),
            reads=[dtraw[pa], dtb[j]], writes=[dt_f[pa]])
        p.do("v", lambda e, C=C, pa=pa: e.tensor_add(
            dt_f[pa][0:128, 0:C], dt_f[pa][0:128, 0:C], sp1[pa][0:128, 0:C]),
            reads=[sp1[pa]], writes=[dt_f[pa]])
        p.do("v", lambda e, C=C, off=off, pa=pa: e.tensor_mul(
            dt_f[pa][0:128, 0:C], dt_f[pa][0:128, 0:C], mask_sb[0:128, off:off + C]),
            reads=[mask_sb], writes=[dt_f[pa]])
        p.do("v", lambda e, C=C, pa=pa: e.tensor_copy(
            dt_b[pa][0:128, 0:C], dt_f[pa][0:128, 0:C]),
            reads=[dt_f[pa]], writes=[dt_b[pa]])
        p.dma("g", dt_hbm[j * 128:(j + 1) * 128, off:off + C], dt_b[pa][:, 0:C],
              reads=[dt_b[pa]])
        p.do("v", lambda e, j=j, pa=pa, s0=s0, W=W: e.tensor_tensor_scan(
            cumf[pa][0:128, 0:W], dt_f[pa][0:128, s0:s0 + W],
            cst["zeros512"][0:128, 0:W],
            cum_col[j][0:128, 0:1], ALU.add, ALU.add),
            reads=[dt_f[pa], cst["zeros512"], cum_col[j]], writes=[cumf[pa]])
        p.do("v", lambda e, j=j, pa=pa, W=W: e.tensor_copy(
            cum_col[j][0:128, 0:1], cumf[pa][0:128, W - 1:W]),
            reads=[cumf[pa]], writes=[cum_col[j]])
        p.do("s", lambda e, pa=pa, W=W: e.copy(cumb[pa][0:128, 0:W], cumf[pa][0:128, 0:W]),
             reads=[cumf[pa]], writes=[cumb[pa]])
        p.dma("g", cum_hbm[j * 128:(j + 1) * 128, c0:c0 + W], cumb[pa][:, 0:W],
              reads=[cumb[pa]])
        if c0 <= spos < c0 + W:
            sc = spos - c0
            p.do("v", lambda e, j=j, sc=sc, pa=pa: e.tensor_mul(
                E_sb[j][0:128, 0:DS], A_sb[j][0:128, 0:DS],
                cumf[pa][0:128, sc:sc + 1].broadcast_to([128, DS])),
                reads=[A_sb[j], cumf[pa]], writes=[E_sb[j]])
            p.do("s", lambda e, j=j: e.activation(
                E_sb[j][0:128, 0:DS], E_sb[j][0:128, 0:DS], AF.Exp),
                reads=[], writes=[E_sb[j]])


def build_mb_program(T_loc, variant):
    p = Prog()
    nc = p.nc
    cfg = m_cfg(variant, T_loc - HALO)
    cst = make_consts(p)
    x_in = nc.dram_tensor("x_in", [DM, T_loc], F32, kind="ExternalInput")
    x_out = nc.dram_tensor("x_out", [DM, T_loc], F32, kind="ExternalOutput")
    S_all = nc.dram_tensor("S_all", [NCORES * DI, DS], F32, kind="ExternalInput")
    E_all = nc.dram_tensor("E_all", [NCORES * DI, DS], F32, kind="ExternalInput")
    z_hbm = nc.dram_tensor("z_scr", [DI, T_loc], BF16, kind="ExternalInput")
    u_hbm = nc.dram_tensor("u_scr", [DI, T_loc], BF16, kind="ExternalInput")
    cum_hbm = nc.dram_tensor("cum_scr", [DI, T_loc], BF16, kind="ExternalInput")
    yl_hbm = nc.dram_tensor("yl_scr", [DI, T_loc], BF16, kind="ExternalInput")
    pj_hbm = nc.dram_tensor("pj_scr", [128, T_loc], BF16, kind="ExternalInput")
    emit_mb(p, cst, T_loc, cfg, x_in, x_out, S_all, E_all, z_hbm, u_hbm,
            cum_hbm, yl_hbm, pj_hbm, m_handles(nc, variant, T_loc))
    return p.finalize(final_waits=[("g", p.count["g"]), ("q", p.count["q"])])


def emit_mb(p, cst, T_loc, cfg, x_in, x_out, S_all, E_all, z_hbm, u_hbm,
            cum_hbm, yl_hbm, pj_hbm, mh):
    nc = p.nc
    var, NPR = cfg["var"], cfg["npr"]
    pstart = cfg["pstart"]
    WF = T_loc - pstart
    H0 = 512
    halves = []
    _o = 0
    while _o < WF:
        halves.append((_o, min(H0, WF - _o)))
        _o += min(H0, WF - _o)
    JG = 4
    ps_big = p.psum(var + "b_psbig", [128, 2560])
    ps_s = [p.psum(var + f"b_pss{i}", [128, 512]) for i in range(3)]

    class _YView(Buf):
        def __init__(self, h, ofs):
            super().__init__(h)
            self._o = ofs
        def __getitem__(self, idx):
            ps, fs = idx
            return self.h[ps, self._o + fs.start:self._o + fs.stop]
    ps_yv = [_YView(ps_big.h, 512 * i) for i in range(4)]
    ps_c2 = _YView(ps_big.h, 2048)

    wout_sb = [p.sbuf(var + f"b_wo{j}", [128, DM], BF16) for j in range(NDI)]
    t_wout = mh["wout"]
    for j in range(NDI):
        p.dma("g", wout_sb[j][:], t_wout[j * 128:(j + 1) * 128, :], writes=[wout_sb[j]])
    alog = load_param_f32(p, mh["alog"], (DI, DS), var + "b_alog")
    dparam = load_param_f32(p, mh["dparam"], (DI, 1), var + "b_dparam")
    selbc = p.sbuf(var + "b_selbc_sb", [NPR, (2 * DS + 1) * 128], BF16)
    p.dma("q", selbc[:], mh["selbc"][:, :], writes=[selbc])
    oh = p.sbuf(var + "b_oh_sb", [128, NCORES], F32)
    p.dma("q", oh[:], mh["oh"][:, :], writes=[oh])

    A_sb = [p.sbuf(var + f"b_A{j}", [128, DS], F32) for j in range(NDI)]
    for j in range(NDI):
        p.do("s", lambda e, j=j: e.activation(A_sb[j][0:128, 0:DS], alog[j][0:128, 0:DS], AF.Exp),
             reads=[alog[j]], writes=[A_sb[j]])
        p.do("v", lambda e, j=j: e.tensor_scalar_mul(A_sb[j][0:128, 0:DS], A_sb[j][0:128, 0:DS], -1.0),
             reads=[], writes=[A_sb[j]])

    H_sb = [p.sbuf(var + f"b_H{j}", [128, DS], F32) for j in range(NDI)]
    Tc = p.sbuf(var + "b_Tc", [128, DS], F32)
    Sr = p.sbuf(var + "b_Sr", [128, DS], F32)
    Er = p.sbuf(var + "b_Er", [128, DS], F32)
    for j in range(NDI):
        p.do("v", lambda e, j=j: e.memset(H_sb[j][0:128, 0:DS], 0.0), writes=[H_sb[j]])
        for r in range(NCORES):
            p.dma("q", Sr[:, 0:DS], S_all[r * DI + j * 128: r * DI + (j + 1) * 128, :],
                  writes=[Sr])
            if r == 0:
                p.do("v", lambda e: e.tensor_copy(Tc[0:128, 0:DS], Sr[0:128, 0:DS]),
                     reads=[Sr], writes=[Tc])
            else:
                p.dma("q", Er[:, 0:DS], E_all[r * DI + j * 128: r * DI + (j + 1) * 128, :],
                      writes=[Er])
                p.do("v", lambda e: e.tensor_mul(Tc[0:128, 0:DS], Tc[0:128, 0:DS],
                                                 Er[0:128, 0:DS]),
                     reads=[Er], writes=[Tc])
                p.do("v", lambda e: e.tensor_add(Tc[0:128, 0:DS], Tc[0:128, 0:DS],
                                                 Sr[0:128, 0:DS]),
                     reads=[Sr], writes=[Tc])
            p.do("v", lambda e, j=j, r=r: e.scalar_tensor_tensor(
                H_sb[j][0:128, 0:DS], Tc[0:128, 0:DS], oh[:, r:r + 1],
                H_sb[j][0:128, 0:DS], ALU.mult, ALU.add),
                reads=[Tc, oh], writes=[H_sb[j]])

    pjf = p.sbuf(var + "b_pjf", [NPR, WF], BF16)
    p.dma("q", pjf[0:NPR, 0:WF], pj_hbm[0:NPR, pstart:T_loc], writes=[pjf])
    cum_g = [p.sbuf(var + f"b_cum{i}", [128, WF], BF16) for i in range(JG)]
    yl_g = [p.sbuf(var + f"b_yl{i}", [128, H0], BF16) for i in range(2)]
    u_g = [p.sbuf(var + f"b_u{i}", [128, H0], BF16) for i in range(2)]
    z_g = [p.sbuf(var + f"b_z{i}", [128, H0], BF16) for i in range(2)]
    zf = [p.sbuf(var + f"b_zf{i}", [128, H0], F32) for i in range(2)]
    crep = [p.sbuf(var + f"b_crep{i}", [128, H0], BF16) for i in range(2)]
    Pt2 = [p.sbuf(var + f"b_P{i}", [128, H0], F32) for i in range(4)]
    yp = [p.sbuf(var + f"b_yp{i}", [128, H0], BF16) for i in range(4)]
    yt = [p.sbuf(var + f"b_yt{i}", [128, H0], F32) for i in range(2)]
    yg_t = [p.sbuf(var + f"b_yg{j}", [128, WF], BF16) for j in range(NDI)]

    for j0 in range(0, NDI, JG):
        for i in range(JG):
            j = j0 + i
            p.dma("q", cum_g[i][:, 0:WF], cum_hbm[j * 128:(j + 1) * 128, pstart:T_loc],
                  writes=[cum_g[i]])

        for (hof, hw) in halves:
            for n in range(DS):
                bp = n % 2
                psC = ps_s[bp]
                p.do("t", lambda e, n=n, hof=hof, hw=hw, psC=psC: e.matmul(
                    psC[0:128, 0:hw],
                    selbc[0:NPR, (DS + n) * 128:(DS + n + 1) * 128],
                    pjf[0:NPR, hof:hof + hw], start=True, stop=True),
                    reads=[selbc, pjf], writes=[psC])
                p.do("s", lambda e, hw=hw, bp=bp, psC=psC: e.copy(
                    crep[bp][0:128, 0:hw], psC[0:128, 0:hw]),
                    reads=[psC], writes=[crep[bp]])
                for i in range(JG):
                    j = j0 + i
                    pa4 = i
                    p.do("s", lambda e, i=i, j=j, n=n, pa4=pa4, hof=hof, hw=hw: e.activation(
                        Pt2[pa4][0:128, 0:hw], cum_g[i][0:128, hof:hof + hw], AF.Exp,
                        bias=0.0, scale=A_sb[j][0:128, n:n + 1]),
                        reads=[cum_g[i], A_sb[j]], writes=[Pt2[pa4]])
                    p.do("v", lambda e, j=j, n=n, pa4=pa4, bp=bp, hw=hw: e.scalar_tensor_tensor(
                        yp[pa4][0:128, 0:hw], Pt2[pa4][0:128, 0:hw],
                        H_sb[j][0:128, n:n + 1], crep[bp][0:128, 0:hw],
                        ALU.mult, ALU.mult),
                        reads=[Pt2[pa4], H_sb[j], crep[bp]], writes=[yp[pa4]])
                    p.do("t", lambda e, n=n, i=i, pa4=pa4, hw=hw: e.matmul(
                        ps_yv[i].h[0:128, ps_yv[i]._o:ps_yv[i]._o + hw],
                        cst["ident_bf"][0:128, 0:128], yp[pa4][0:128, 0:hw],
                        start=(n == 0), stop=(n == DS - 1)),
                        reads=[cst["ident_bf"], yp[pa4]], writes=[ps_yv[i]])
            for i in range(JG):
                j = j0 + i
                ya = i % 2
                p.dma("g", z_g[ya][:, 0:hw],
                      z_hbm[j * 128:(j + 1) * 128, pstart + hof:pstart + hof + hw],
                      writes=[z_g[ya]],
                      **({"allow_slow_non_contiguous": True} if hw < 8 else {}))
                p.dma("g", u_g[ya][:, 0:hw],
                      u_hbm[j * 128:(j + 1) * 128, pstart + hof:pstart + hof + hw],
                      writes=[u_g[ya]],
                      **({"allow_slow_non_contiguous": True} if hw < 8 else {}))
                p.dma("q", yl_g[ya][:, 0:hw],
                      yl_hbm[j * 128:(j + 1) * 128, pstart + hof:pstart + hof + hw],
                      writes=[yl_g[ya]],
                      **({"allow_slow_non_contiguous": True} if hw < 8 else {}))
                p.do("s", lambda e, ya=ya, hw=hw: e.activation(
                    zf[ya][0:128, 0:hw], z_g[ya][0:128, 0:hw], AF.Silu),
                    reads=[z_g[ya]], writes=[zf[ya]])
                p.do("v", lambda e, i=i, ya=ya, hw=hw: e.tensor_add(
                    yt[ya][0:128, 0:hw], yl_g[ya][0:128, 0:hw],
                    ps_yv[i].h[0:128, ps_yv[i]._o:ps_yv[i]._o + hw]),
                    reads=[yl_g[ya], ps_yv[i]], writes=[yt[ya]])
                p.do("v", lambda e, j=j, ya=ya, hw=hw: e.scalar_tensor_tensor(
                    yt[ya][0:128, 0:hw], u_g[ya][0:128, 0:hw],
                    dparam[j][0:128, 0:1], yt[ya][0:128, 0:hw],
                    ALU.mult, ALU.add),
                    reads=[u_g[ya], dparam[j]], writes=[yt[ya]])
                p.do("v", lambda e, j=j, ya=ya, hof=hof, hw=hw: e.tensor_mul(
                    yg_t[j][0:128, hof:hof + hw], yt[ya][0:128, 0:hw],
                    zf[ya][0:128, 0:hw]),
                    reads=[yt[ya], zf[ya]], writes=[yg_t[j]])

    x_t = [p.sbuf(var + f"b_x{i}", [128, 512], F32) for i in range(NDM)]
    for off, C in chunk_plan(T_loc):
        c0 = max(off, pstart)
        W = off + C - c0
        s0 = c0 - off
        for i in range(NDM):
            p.dma("q", x_t[i][:, 0:C], x_in[i * 128:(i + 1) * 128, off:off + C],
                  writes=[x_t[i]])
        for m in range(NDM):
            psm = ps_s[m % 2]
            p.do_group("t", [
                (lambda e, m=m, k=k, W=W, c0=c0, psm=psm: e.matmul(
                    psm[:, 0:W], wout_sb[k][0:128, m * 128:(m + 1) * 128],
                    yg_t[k][0:128, c0 - pstart:c0 - pstart + W],
                    start=(k == 0), stop=(k == NDI - 1)))
                for k in range(NDI)],
                reads=wout_sb + yg_t, writes=[psm])
            p.do("v", lambda e, m=m, s0=s0, W=W, psm=psm: e.tensor_add(
                x_t[m][0:128, s0:s0 + W], x_t[m][0:128, s0:s0 + W], psm[:, 0:W]),
                reads=[psm], writes=[x_t[m]])
            p.dma("g", x_out[m * 128:(m + 1) * 128, off:off + C], x_t[m][:, 0:C],
                  reads=[x_t[m]])

